# version 01
# baseline (speedup 1.0000x reference)
import contextlib
import numpy as np

import concourse.bass as bass
import concourse.mybir as mybir
from concourse.bass_utils import run_bass_kernel_spmd

F32 = mybir.dt.float32
BF16 = mybir.dt.bfloat16
I32 = mybir.dt.int32
AF = mybir.ActivationFunctionType
ALU = mybir.AluOpType
AX = mybir.AxisListType.X

SEQ = 4096
DM = 2048
NH = 32
DST = 64
XCH = 512
NG = 8
CCH = 2048
CONVC = XCH + XCH + CCH
NCT = CONVC // 128
DIN = 2048
CH = 128
NCHUNK = SEQ // CH
SC = 4
NSC = NCHUNK // SC
TOKH = 2048
HF = 8192
GU = 2 * HF
EPS_LN = 1e-5
EPS_RMS = 1e-5
EPS_FFN = float(np.finfo(np.float32).eps)
NEG = -3.0e38

ENGINES = ("sync", "vector", "scalar", "tensor", "gpsimd")
DMA_ENG = {"sync", "gpsimd", "scalar"}


class Prog:

    NCH = 4

    def __init__(self):
        self.items = []
        self.cnt = {}
        for e in ENGINES:
            self.cnt[e] = 0
        for e in ("sync", "scalar", "gpsimd"):
            for i in range(self.NCH):
                self.cnt[f"{e}_c{i}"] = 0
        self.last_chan = {}
        self._rr = {e: 0 for e in ("sync", "scalar", "gpsimd")}

    @staticmethod
    def _flat(wait):
        out = []
        for t in wait:
            if t is None:
                continue
            if isinstance(t, list):
                out.extend(x for x in t if x is not None)
            else:
                out.append(t)
        return tuple(out)

    def op(self, eng, fn, wait=(), dma=False, silent=False, stream=None, manual=False):
        if dma:
            assert not silent
            if stream is None:
                ch = self._rr[eng]
                self._rr[eng] = (ch + 1) % self.NCH
            else:
                ch = hash(stream) % self.NCH
            semname = f"{eng}_c{ch}"
            self.cnt[semname] += 16
            tok = (semname, self.cnt[semname])
            prev = self.last_chan.get(semname)
            self.last_chan[semname] = tok
            self.items.append((eng, fn, self._flat(list(wait) + [prev]), tok, 16,
                               semname, manual))
            return tok
        if silent:
            self.items.append((eng, fn, self._flat(wait), None, 0, eng, False))
            return None
        self.cnt[eng] += 1
        tok = (eng, self.cnt[eng])
        self.items.append((eng, fn, self._flat(wait), tok, 1, eng, False))
        return tok

    def barrier(self):
        toks = [(s, c) for s, c in self.cnt.items() if c > 0]
        new = []
        for e in ENGINES:
            new.append(self.op(e, lambda eng: eng.nop(), wait=toks))
        return new

    def replay(self, eng_name, eng, sems):
        hw = {}
        for item in self.items:
            item_eng, fn, waits, tok, inc, semname, manual = item
            if item_eng != eng_name:
                continue
            for (we, wv) in waits:
                if hw.get(we, 0) < wv:
                    eng.wait_ge(sems[we], wv)
                    hw[we] = wv
            if manual:
                fn(eng, sems[semname])
            else:
                inst = fn(eng)
                if inst is not None and inc:
                    inst.then_inc(sems[semname], inc)


def ceil_div(a, b):
    return -(-a // b)



def make_in_maps(L0_attn_u, L0_attn_W_in, L0_attn_conv_w, L0_attn_conv_b,
                 L0_attn_dt_bias, L0_attn_A_log, L0_attn_D, L0_attn_norm_weight,
                 L0_attn_W_out, L0_ffn_w_gate_up, L0_ffn_w_proj, nonce):
    c32 = lambda x: np.ascontiguousarray(x, dtype=np.float32)
    u = c32(L0_attn_u)
    W_in = c32(L0_attn_W_in)
    conv_w = c32(L0_attn_conv_w)
    conv_b = c32(L0_attn_conv_b)
    dt_bias = c32(L0_attn_dt_bias)
    A_log = c32(L0_attn_A_log)
    Dsk = c32(L0_attn_D)
    gamma = c32(L0_attn_norm_weight)
    W_out = c32(L0_attn_W_out)
    Wgu = c32(L0_ffn_w_gate_up)
    Wp = c32(L0_ffn_w_proj)

    idb = np.eye(128, dtype=np.float32).astype(np.dtype("bfloat16") if False else np.float32)
    idf = np.eye(128, dtype=np.float32)
    j = np.arange(128)
    mask1 = np.where(j[:, None] <= j[None, :], 0.0, NEG).astype(np.float32)
    mask4 = np.tile(mask1, (1, 4))

    maps = []
    for core in range(8):
        s, par = core // 2, core % 2
        zc = par * 2048
        xc = par * XCH
        bc = 1024 + par * XCH
        cc = 2048 + par * CCH
        dtc = 4096 + 6144 + par * NH
        Wz = W_in[:, 4096 + 0:0][:, :0]
        Wz = W_in[:, zc:zc + DIN]
        Wxbc = np.concatenate([
            W_in[:, 4096 + xc:4096 + xc + XCH],
            W_in[:, 4096 + bc:4096 + bc + XCH],
            W_in[:, 4096 + cc:4096 + cc + CCH],
            W_in[:, dtc:dtc + NH],
        ], axis=1)
        rows = np.concatenate([
            np.arange(xc, xc + XCH),
            np.arange(bc, bc + XCH),
            np.arange(cc, cc + CCH),
        ])
        cw = conv_w[rows].reshape(NCT, 128, 4).transpose(1, 0, 2).reshape(128, NCT * 4)
        cb = conv_b[rows].reshape(NCT, 128).T.copy()
        gam = gamma[par * DIN:(par + 1) * DIN].reshape(16, 128).T.copy()
        maps.append({
            "u": np.ascontiguousarray(u[s]),
            "wz": np.ascontiguousarray(Wz),
            "wxbc": np.ascontiguousarray(Wxbc),
            "wout": np.ascontiguousarray(W_out[par * DIN:(par + 1) * DIN]),
            "wgu_sh": np.ascontiguousarray(Wgu[par * HF:(par + 1) * HF]),
            "wp_sh": np.ascontiguousarray(Wp[par * 1024:(par + 1) * 1024]),
            "convw": np.ascontiguousarray(cw),
            "convb": np.ascontiguousarray(cb),
            "gam": np.ascontiguousarray(gam),
            "dtb": np.ascontiguousarray(dt_bias[par * NH:(par + 1) * NH].reshape(NH, 1)),
            "alog": np.ascontiguousarray(A_log[par * NH:(par + 1) * NH].reshape(NH, 1)),
            "dsk": np.ascontiguousarray(Dsk[par * NH:(par + 1) * NH].reshape(NH, 1)),
            "idf": idf,
            "mask4": mask4,
            "parity": np.array([[par]], np.int32),
            "nonce": np.array([[nonce]], np.int32),
        })
    return maps


class Arena:

    def __init__(self, handle, total):
        self.h = handle
        self.total = total
        self.off = 0

    def reset(self):
        self.off = 0

    def take(self, rows, cols, dtype):
        unit = 2 if dtype == F32 else 1
        n = cols * unit
        n = (n + 63) & ~63
        assert self.off + n <= self.total, f"arena overflow {self.off}+{n}>{self.total}"
        ap = self.h[:, self.off:self.off + n]
        self.off += n
        if dtype == F32:
            ap = ap.bitcast(F32)
        ap = ap[0:rows, 0:cols]
        return ap


class KB:
    def __init__(self, dumps=(), taps_on=False):
        self.taps_on = taps_on
        self.tap_map = {}
        self.tap_row = 0
        self.tap_row_b = 0
        self._zev = {}
        self._xev = {}
        self.dumps = set(dumps)
        self.nc = bass.Bass("TRN2", target_bir_lowering=False)
        self.P = Prog()
        self.es = contextlib.ExitStack()

    def declare(self):
        nc = self.nc
        d = nc.dram_tensor
        self.u = d("u", [SEQ, DM], F32, kind="ExternalInput")
        self.wz = d("wz", [DM, DIN], F32, kind="ExternalInput")
        self.wxbc = d("wxbc", [DM, CONVC + NH], F32, kind="ExternalInput")
        self.wout = d("wout", [DIN, DM], F32, kind="ExternalInput")
        self.wgu_sh = d("wgu_sh", [HF, DM], F32, kind="ExternalInput")
        self.wp_sh = d("wp_sh", [1024, HF], F32, kind="ExternalInput")
        self.convw = d("convw", [128, NCT * 4], F32, kind="ExternalInput")
        self.convb = d("convb", [128, NCT], F32, kind="ExternalInput")
        self.gam = d("gam", [128, 16], F32, kind="ExternalInput")
        self.dtb = d("dtb", [NH, 1], F32, kind="ExternalInput")
        self.alog = d("alog", [NH, 1], F32, kind="ExternalInput")
        self.dsk = d("dsk", [NH, 1], F32, kind="ExternalInput")
        self.idf = d("idf", [128, 128], F32, kind="ExternalInput")
        self.mask4 = d("mask4", [128, 512], F32, kind="ExternalInput")
        self.parity = d("parity", [1, 1], I32, kind="ExternalInput")
        self.noncet = d("nonce", [1, 1], I32, kind="ExternalInput")
        self.outt = d("outp", [TOKH, DM], F32, kind="ExternalOutput")

        self.XBCS = d("XBCS", [CONVC, SEQ + 3], BF16)
        self.ZS = d("ZS", [SEQ, DIN], BF16)
        self.DTS = d("DTS", [NH, SEQ], F32)
        self.HS = d("HS", [HF, TOKH], BF16)
        self.LN1S = d("LN1S", [TOKH, DM], BF16)
        self.WOS = d("WOS", [DIN, DM], BF16)
        self.BRS = d("BRS", [64, 128], F32)
        self.RSS = d("RSS", [1, TOKH], F32)
        self.WQGU = d("WQGU", [GU, DM], BF16, addr_space="Shared")
        self.WQP = d("WQP", [DM, HF], BF16, addr_space="Shared")
        self.SHA = d("SHA", [2, SEQ, DM], F32, addr_space="Shared")
        self.FLG = d("FLG", [2, 32], I32, addr_space="Shared")

        self.DBG = d("DBG", [4096, 8192], F32)
        self.DBGB = d("DBGB", [2048, 8192], BF16)
        self.dump_t = {}
        for name in self.dumps:
            src = getattr(self, name)
            self.dump_t[name] = d("dump_" + name, list(src.shape), src.dtype
                                  if hasattr(src, "dtype") else F32, kind="ExternalOutput")

    def alloc_sbuf(self):
        nc, es = self.nc, self.es
        ARN_COLS = 131072
        self.AR = Arena(es.enter_context(nc.sbuf_tensor("ARN", [128, 65536], BF16)), 65536)
        t = lambda n, s, dt: es.enter_context(nc.sbuf_tensor(n, s, dt))
        self.IDB = t("IDB", [128, 128], BF16)
        self.IDF = t("IDFs", [128, 128], F32)
        self.MSK = t("MSK", [128, 512], F32)
        self.ONE1 = t("ONE1", [1, 128], F32)
        self.ONEB = t("ONEB", [128, 1], BF16)
        self.CONVW = t("CONVWs", [128, NCT * 4], F32)
        self.CONVB = t("CONVBs", [128, NCT], F32)
        self.GCOL = t("GCOL", [128, 16], F32)
        self.DTBC = t("DTBC", [NH, 1], F32)
        self.ACOL = t("ACOL", [NH, 1], F32)
        self.DBC = t("DBC", [128, NH], F32)
        self.DSKR = t("DSKR", [1, NH], F32)
        self.PARN = t("PARN", [1, 4], I32)
        self.CUM = t("CUM", [NH, 128], F32)
        self.DTF = t("DTF", [NH, 128], F32)
        self.DEC = t("DEC", [NH, 128], F32)
        self.EXPC = t("EXPC", [NH, 128], F32)
        self.TCOL = t("TCOL", [128, 96], F32)
        self.NEGT = t("NEGT", [128, 32], F32)
        self.CDECB = t("CDECB", [64, NH], F32)
        self.SPT = t("SPT", [NH, 384], F32)
        self.BR = t("BR", [1, 8192], F32)
        self.HA = t("HA", [64, NH * 64], F32)
        self.HB = t("HB", [64, NH * 64], F32)
        self.HBF = t("HBF", [128, NH * 64], BF16)
        self.COLS = t("COLS", [128, 16], F32)
        self.RST2 = t("RST2", [128, 16], F32)
        self.FLT = t("FLT", [1, 1], I32)
        self.DTCH = t("DTCH", [NH, 128], F32)
        self.PS = [es.enter_context(nc.psum_tensor(f"PS{i}", [128, 512], F32))
                   for i in range(8)]

    def psum_bf(self, i):
        return self.PS[i].ap().bitcast(BF16)

    def build(self, stop_after=None):
        self.declare()
        with self.es:
            self.alloc_sbuf()
            P = self.P
            self.tok_regs = None
            phases = ["setup", "quant", "inproj", "wosprep", "ssd", "ln1", "f1", "f2"]
            for ph in phases:
                getattr(self, "ph_" + ph)()
                P.barrier()
                if stop_after == ph:
                    break
            self.ph_dumps()
            P.barrier()
            sems = {}
            for name in self.P.cnt.keys():
                sems[name] = self.es.enter_context(self.nc.semaphore("sem_" + name))
            self.sems = sems
            self.psem = self.es.enter_context(self.nc.semaphore("psem"))
            block = self.es.enter_context(self.nc.Block())
            P = self.P

            @block.sync
            def _(sync):
                P.replay("sync", sync, sems)

            @block.vector
            def _(vector):
                P.replay("vector", vector, sems)

            @block.scalar
            def _(scalar):
                P.replay("scalar", scalar, sems)

            @block.tensor
            def _(tensor):
                P.replay("tensor", tensor, sems)

            @block.gpsimd
            def _(gpsimd):
                self._greg_ctx = contextlib.ExitStack()
                with self._greg_ctx:
                    regs = {}
                    for rn in ["rpar", "roth", "rnon", "rflag", "rcond", "rcnt",
                               "ra", "rb"]:
                        regs[rn] = self._greg_ctx.enter_context(gpsimd.register(rn))
                    self.regs = regs
                    P.replay("gpsimd", gpsimd, sems)

        return self.nc

    def ph_setup(self):
        P = self.P
        s = lambda fn: P.op("sync", fn, dma=True)
        t_idf = s(lambda e: e.dma_start(out=self.IDF[:], in_=self.idf.ap()))
        t_msk = s(lambda e: e.dma_start(out=self.MSK[:], in_=self.mask4.ap()))
        t_cw = s(lambda e: e.dma_start(out=self.CONVW[:], in_=self.convw.ap()))
        t_cb = s(lambda e: e.dma_start(out=self.CONVB[:], in_=self.convb.ap()))
        t_g = s(lambda e: e.dma_start(out=self.GCOL[:], in_=self.gam.ap()))
        t_dtb = s(lambda e: e.dma_start(out=self.DTBC[:], in_=self.dtb.ap()))
        t_al = s(lambda e: e.dma_start(out=self.COLS[0:NH, 0:1], in_=self.alog.ap()))
        t_dr = s(lambda e: e.dma_start(out=self.DSKR[:], in_=self.dsk.ap().rearrange("h o -> o h").rearrange("o h -> (o h)").rearrange("(o h) -> o h", o=1)))
        t_par = s(lambda e: e.dma_start(out=self.PARN[0:1, 0:1], in_=self.parity.ap()))
        t_non = s(lambda e: e.dma_start(out=self.PARN[0:1, 1:2], in_=self.noncet.ap()))
        self.t_msk = t_msk
        t_one = P.op("vector", lambda e: e.memset(self.ONE1[:], 1.0))
        self.t_zero_col = P.op("vector", lambda e: e.memset(self.COLS[:, 2:3], 0.0))
        self.t_eps_ln = P.op("vector", lambda e: e.memset(self.COLS[:, 9:10], EPS_LN))
        self.t_eps_ffn = P.op("vector", lambda e: e.memset(self.COLS[:, 10:11], EPS_FFN))
        t_oneb = P.op("vector", lambda e: e.memset(self.ONEB[:], 1.0))
        t_idb = P.op("vector", lambda e: e.tensor_copy(self.IDB[:], self.IDF[:]), wait=[t_idf])
        t_e = P.op("scalar", lambda e: e.activation(self.COLS[0:NH, 1:2], self.COLS[0:NH, 0:1], AF.Exp), wait=[t_al])
        t_acol = P.op("vector", lambda e: e.tensor_scalar_mul(self.ACOL[:], self.COLS[0:NH, 1:2], -1.0), wait=[t_e])
        t_dmm = P.op("tensor", lambda e: e.matmul(self.PS[0][:, 0:NH], self.ONE1[:], self.DSKR[:],
                                                  start=True, stop=True), wait=[t_one, t_dr])
        t_dbc = P.op("vector", lambda e: e.tensor_copy(self.DBC[:], self.PS[0][:, 0:NH]), wait=[t_dmm])
        def setup_regs(e):
            r = self.regs
            e.reg_load(r["rpar"], self.PARN[0:1, 0:1])
            e.reg_load(r["rnon"], self.PARN[0:1, 1:2])
            e.reg_mov(r["rcnt"], 0)
            return e.reg_alu(r["roth"], 1, r["rpar"], ALU.subtract)
        t_regs = P.op("gpsimd", setup_regs, wait=[t_par, t_non])
        self.toks = dict(idb=t_idb, acol=t_acol, dbc=t_dbc, regs=t_regs, cw=t_cw,
                         cb=t_cb, gcol=t_g, dtbc=t_dtb, oneb=t_oneb, one=t_one,
                         idf=t_idf, msk=t_msk)

    def tap(self, name, ap, wait, cast=False):
        if not getattr(self, "taps_on", False):
            return None
        isbf = ap.dtype == BF16
        dst = self.DBGB if isbf else self.DBG
        rowkey = "tap_row_b" if isbf else "tap_row"
        row = getattr(self, rowkey)
        P_, F_ = ap.shape[0], int(np.prod(ap.shape[1:]))
        self.tap_map[name] = (row, P_, F_, isbf)
        setattr(self, rowkey, row + P_)
        return self.P.op("gpsimd", (lambda a=ap, r=row, p=P_, f=F_, d_=dst: lambda e: e.dma_start(
            out=d_.ap()[r:r + p, 0:f], in_=a))(), wait=wait, dma=True)

    def par_dma(self, fn0, fn1):
        def fn(e, sem):
            rg = self.regs
            with e.If_eq(rg["rpar"], 0):
                fn0(e).then_inc(sem, 16)
            with e.Else():
                fn1(e).then_inc(sem, 16)
            return None
        return fn

    def flag_write(self, idx, wait):
        fn = self.par_dma(
            lambda e: e.dma_start(out=self.FLG.ap()[0:1, idx:idx + 1], in_=self.PARN[0:1, 1:2]),
            lambda e: e.dma_start(out=self.FLG.ap()[1:2, idx:idx + 1], in_=self.PARN[0:1, 1:2]))
        return self.P.op("gpsimd", fn, wait=list(wait) + [self.toks["regs"]], dma=True, manual=True, stream="flags")

    def flag_poll(self, idx, extra=0):
        P = self.P

        def fn2(e):
            rg = self.regs
            for par_i, src_ap in ((0, self.FLG.ap()[1:2, idx:idx + 1]),
                                  (1, self.FLG.ap()[0:1, idx:idx + 1])):
                guard = e.If_eq(rg["rpar"], par_i) if par_i == 0 else e.Else()
                with guard:
                    e.reg_mov(rg["rcond"], 1)
                    with e.While(rg["rcond"]):
                        e.dma_start(out=self.FLT[:], in_=src_ap).then_inc(self.psem, 16)
                        e.reg_add(rg["rcnt"], rg["rcnt"], 16)
                        e.wait_ge(self.psem, rg["rcnt"])
                        e.reg_load(rg["rflag"], self.FLT[0:1, 0:1])
                        e.reg_alu(rg["rcond"], rg["rflag"], rg["rnon"], ALU.bitwise_xor)
            return e.nop()
        return P.op("gpsimd", fn2, wait=[self.toks["regs"]])

    def ph_quant(self):
        P, AR = self.P, self.AR
        AR.reset()
        W32 = [AR.take(128, 2048, F32) for _ in range(2)]
        WBF = [AR.take(128, 2048, BF16) for _ in range(2)]
        GS = [AR.take(128, 32, F32) for _ in range(2)]
        GS2 = [AR.take(128, 32, F32) for _ in range(2)]
        SCB = [AR.take(128, 32, BF16) for _ in range(2)]
        THR = [AR.take(128, 32, F32) for _ in range(2)]
        M = [AR.take(128, 2048, F32) for _ in range(2)]
        ABSQ = [AR.take(128, 2048, F32) for _ in range(2)]
        SGN = [AR.take(128, 2048, BF16) for _ in range(2)]
        TT = [AR.take(128, 2048, F32) for _ in range(2)]
        WH = [AR.take(128, 2048, BF16) for _ in range(2)]
        b3 = lambda ap: ap.rearrange("p (g o) -> p g o", o=1).to_broadcast([128, 32, 64])
        v3 = lambda ap: ap.rearrange("p (g k) -> p g k", g=32)

        units = []
        for r in range(HF // 128):
            units.append(("gu", self.wgu_sh.ap()[r * 128:(r + 1) * 128, :],
                          16777216, r * 262144, self.WQGU, 2048))
        for r in range(1024 // 128):
            for cc in range(4):
                units.append(("p", self.wp_sh.ap()[r * 128:(r + 1) * 128, cc * 2048:(cc + 1) * 2048],
                              8388608, r * 128 * 8192 + cc * 2048, self.WQP, 8192))
        last = {}
        t_whd_last = None
        for i, (kind, src, parmul, off, dst, dstride) in enumerate(units):
            u = i % 2
            t_load = P.op("sync", (lambda s_=src, w=W32[u]: lambda e: e.dma_start(out=w, in_=s_))(),
                          wait=[last.get(("cast", u))], dma=True)
            t_cast = P.op("gpsimd", (lambda w=W32[u], o=WBF[u]: lambda e: e.tensor_copy(o, w))(),
                          wait=[t_load, last.get(("gs", u)), last.get(("m", u)), last.get(("sgn", u))])
            t_gs = P.op("vector", (lambda w=WBF[u], o=GS[u]: lambda e: e.tensor_reduce(
                o.rearrange("p (g o) -> p g o", o=1), v3(w), AX, ALU.add, apply_absolute_value=True))(),
                wait=[t_cast, last.get(("sc1", u))])
            t_sc1 = P.op("vector", (lambda a=GS[u], o=GS2[u]: lambda e: e.tensor_scalar(
                o, a, 1.0 / 64, 1e-8, ALU.mult, ALU.max))(), wait=[t_gs, last.get(("scb", u))])
            t_scb = P.op("gpsimd", (lambda a=GS2[u], o=SCB[u]: lambda e: e.tensor_copy(o, a))(),
                         wait=[t_sc1, last.get(("thr", u)), last.get(("tt", u))])
            t_thr = P.op("vector", (lambda a=SCB[u], o=THR[u]: lambda e: e.tensor_scalar_mul(
                o, a, float(0.5 + 2.0 ** -10)))(), wait=[t_scb, last.get(("m", u))])
            t_abs = P.op("scalar", (lambda w=WBF[u], o=ABSQ[u]: lambda e: e.activation(
                o, w, AF.Abs))(), wait=[t_cast, last.get(("absr", u))])
            t_m = P.op("vector", (lambda a=ABSQ[u], th=THR[u], o=M[u]: lambda e: e.tensor_tensor(
                v3(o), v3(a), b3(th), ALU.is_gt))(),
                wait=[t_thr, t_abs, last.get(("tt", u))])
            last[("absr", u)] = t_m
            t_sgn = P.op("scalar", (lambda w=WBF[u], o=SGN[u]: lambda e: e.sign(o, w))(),
                         wait=[t_cast, last.get(("wh", u))])
            t_tt = P.op("gpsimd", (lambda m=M[u], sc=SCB[u], o=TT[u]: lambda e: e.tensor_tensor(
                v3(o), v3(m), b3(sc), ALU.mult))(), wait=[t_m, last.get(("wh", u))])
            t_wh = P.op("vector", (lambda t=TT[u], sg=SGN[u], o=WH[u]: lambda e: e.tensor_tensor(
                o, t, sg, ALU.mult))(), wait=[t_tt, t_sgn, last.get(("whd", u))])

            whd_fn = self.par_dma(
                (lambda of=off, d=dst, ds=dstride, w=WH[u]: lambda e: e.dma_start(
                    out=bass.AP(d, of, [[ds, 128], [1, 2048]]), in_=w))(),
                (lambda pm=parmul, of=off, d=dst, ds=dstride, w=WH[u]: lambda e: e.dma_start(
                    out=bass.AP(d, pm + of, [[ds, 128], [1, 2048]]), in_=w))())
            t_whd = P.op("gpsimd", whd_fn, wait=[t_wh, self.toks["regs"]], dma=True, manual=True, stream="whd")
            last.update({("cast", u): t_cast, ("gs", u): t_gs, ("sc1", u): t_sc1,
                         ("scb", u): t_scb, ("thr", u): t_thr, ("m", u): t_m,
                         ("sgn", u): t_sgn, ("tt", u): t_tt, ("wh", u): t_wh,
                         ("whd", u): t_whd})
            t_whd_last = t_whd
        self.t_quant_flag = self.flag_write(0, wait=[t_whd_last])

    def ph_inproj(self):
        P, AR = self.P, self.AR
        AR.reset()
        UFM = AR.take(128, 32768, BF16)
        WZQ = AR.take(128, 8192, BF16)
        WXB = [AR.take(128, 2048, BF16) for _ in range(2)]
        USTG = AR.take(128, 2048, F32)
        UCST = AR.take(128, 2048, BF16)
        EZ = [AR.take(128, 512, BF16) for _ in range(2)]
        DTE = AR.take(32, 512, F32)
        WOF = AR.take(128, 2048, F32)
        WOB = AR.take(128, 2048, BF16)
        HZ = AR.take(128, 4, BF16)
        self._wos = (WOF, WOB)
        ufm = lambda k: UFM[:, k * 2048:(k + 1) * 2048]
        TBF = [self.psum_bf(6), self.psum_bf(7)]

        t_hz = P.op("vector", lambda e: e.memset(HZ[:], 0.0))
        for j in range(NCT):
            P.op("gpsimd", (lambda jj=j: lambda e: e.dma_start(
                out=self.XBCS.ap()[jj * 128:(jj + 1) * 128, 0:3], in_=HZ[:, 0:3]))(),
                wait=[t_hz], dma=True)

        last_tp = {0: None, 1: None}
        last_ez = {0: None, 1: None}
        last_wzmm = None
        last_wxmm = {0: None, 1: None}
        last_ps01 = {0: None, 1: None}
        last_ps23 = {0: None, 1: None}
        last_dte = None
        t_cast_prev = None
        t_lastev_prev = None
        for hf in range(2):
            tok_ev = {}
            for tt in range(16):
                tg = hf * 16 + tt
                t_u = P.op("sync", (lambda r=tg: lambda e: e.dma_start(
                    out=USTG, in_=self.u.ap()[r * 128:(r + 1) * 128, :]))(),
                    wait=[t_cast_prev], dma=True)
                t_cast = P.op("vector", lambda e: e.tensor_copy(UCST, USTG),
                              wait=[t_u, t_lastev_prev])
                t_cast_prev = t_cast
                for k in range(16):
                    sl = k % 2
                    t_tp = P.op("tensor", (lambda kk=k, p=TBF[sl]: lambda e: e.transpose(
                        p[0:128, 0:128], UCST[:, kk * 128:(kk + 1) * 128], self.IDB[:]))(),
                        wait=[t_cast, self.toks["idb"], last_tp[sl]])
                    eng = "vector" if k % 2 else "scalar"
                    if eng == "vector":
                        fn = (lambda kk=k, p=TBF[sl], t_=tt: lambda e: e.tensor_copy(
                            ufm(kk)[:, t_ * 128:(t_ + 1) * 128], p[0:128, 0:128]))()
                    else:
                        fn = (lambda kk=k, p=TBF[sl], t_=tt: lambda e: e.copy(
                            ufm(kk)[:, t_ * 128:(t_ + 1) * 128], p[0:128, 0:128]))()
                    tok_ev[(tt, k)] = P.op(eng, fn, wait=[t_tp])
                    last_tp[sl] = tok_ev[(tt, k)]
                t_lastev_prev = tok_ev[(tt, 15)]

            for q in range(4):
                t_wz = P.op("gpsimd", (lambda qq=q: lambda e: e.dma_start(
                    out=WZQ.rearrange("p (a f) -> p a f", a=16),
                    in_=self.wz.ap()[:, qq * 512:(qq + 1) * 512].rearrange(
                        "(a p) f -> p a f", p=128)))(),
                    wait=[last_wzmm], dma=True)
                for tt in range(16):
                    pp = self.PS[tt % 2]
                    ppev = {}
                    t_mm = None
                    for k in range(16):
                        t_mm = P.op("tensor", (lambda kk=k, t_=tt, p=pp: lambda e: e.matmul(
                            p[:, 0:512], ufm(kk)[:, t_ * 128:(t_ + 1) * 128],
                            WZQ[:, kk * 512:(kk + 1) * 512],
                            start=(kk == 0), stop=(kk == 15)))(),
                            wait=([t_wz, tok_ev[(tt, k)]]
                                  + ([last_ps01[tt % 2]] if k == 0 else [])),
                            silent=(k < 15))
                    last_wzmm = t_mm
                    t_ev = P.op("vector", (lambda p=pp, u=EZ[tt % 2]: lambda e: e.tensor_copy(u, p[:, 0:512]))(),
                                wait=[t_mm, last_ez[tt % 2]])
                    last_ps01[tt % 2] = t_ev
                    t_dma = P.op("sync", (lambda u=EZ[tt % 2], r=hf * 2048 + tt * 128, qq=q:
                                 lambda e: e.dma_start(out=self.ZS.ap()[r:r + 128, qq * 512:(qq + 1) * 512], in_=u))(),
                                 wait=[t_ev], dma=True)
                    last_ez[tt % 2] = t_dma

            for j in range(NCT + 1):
                w = 128 if j < NCT else NH
                t_wx = P.op("gpsimd", (lambda jj=j, ww=w, b=WXB[j % 2]: lambda e: e.dma_start(
                    out=b[:, 0:16 * ww].rearrange("p (a f) -> p a f", a=16),
                    in_=self.wxbc.ap()[:, jj * 128:jj * 128 + ww].rearrange(
                        "(a p) f -> p a f", p=128)))(),
                    wait=[last_wxmm[j % 2]], dma=True)
                for tb in range(4):
                    pp = self.PS[2 + tb % 2]
                    t_mm = None
                    for k in range(16):
                        t_mm = P.op("tensor", (lambda kk=k, b=WXB[j % 2], ww=w, t_=tb, p=pp: lambda e: e.matmul(
                            p[0:ww, 0:512], b[:, kk * ww:(kk + 1) * ww],
                            ufm(kk)[:, t_ * 512:(t_ + 1) * 512],
                            start=(kk == 0), stop=(kk == 15)))(),
                            wait=([t_wx, tok_ev[(4 * tb + 3, k)]]
                                  + ([last_ps23[tb % 2]] if k == 0 else [])),
                            silent=(k < 15))
                    last_wxmm[j % 2] = t_mm
                    if j < NCT:
                        t_ev = P.op("scalar", (lambda p=pp, u=EZ[tb % 2]: lambda e: e.copy(u, p[0:128, 0:512]))(),
                                    wait=[t_mm, last_ez[tb % 2]])
                        t_dma = P.op("sync", (lambda u=EZ[tb % 2], r=j * 128, c=3 + hf * 2048 + tb * 512:
                                     lambda e: e.dma_start(out=self.XBCS.ap()[r:r + 128, c:c + 512], in_=u))(),
                                     wait=[t_ev], dma=True)
                        last_ez[tb % 2] = t_dma
                        last_ps23[tb % 2] = t_ev
                    else:
                        t_ev = P.op("vector", (lambda p=pp: lambda e: e.tensor_copy(DTE, p[0:NH, 0:512]))(),
                                    wait=[t_mm, last_dte])
                        t_dma = P.op("sync", (lambda c=hf * 2048 + tb * 512:
                                     lambda e: e.dma_start(out=self.DTS.ap()[:, c:c + 512], in_=DTE))(),
                                     wait=[t_ev], dma=True)
                        last_dte = t_dma
                        last_ps23[tb % 2] = t_ev

    def ph_wosprep(self):
        P = self.P
        WOF, WOB = self._wos
        tprev = None
        for kk in range(16):
            t_l = P.op("sync", (lambda r=kk: lambda e: e.dma_start(
                out=WOF, in_=self.wout.ap()[r * 128:(r + 1) * 128, :]))(),
                wait=[tprev], dma=True)
            t_s = P.op("vector", (lambda r=kk: lambda e: e.tensor_scalar(
                WOB, WOF, self.GCOL[:, r:r + 1], None, ALU.mult))(),
                wait=[t_l, self.toks["gcol"], tprev])
            t_d = P.op("sync", (lambda r=kk: lambda e: e.dma_start(
                out=self.WOS.ap()[r * 128:(r + 1) * 128, :], in_=WOB))(),
                wait=[t_s], dma=True)
            tprev = t_d

    def ph_ssd(self):
        P, AR = self.P, self.AR
        AR.reset()
        XC = AR.take(128, 6 * 520, BF16)
        CFM = AR.take(128, 16 * 512, BF16)
        XFM = AR.take(128, 4 * 512, BF16)
        BFM = AR.take(128, 4 * 512, BF16)
        BFMD = AR.take(128, 8 * 512, BF16)
        DXT = AR.take(128, 2048, BF16)
        TY = [AR.take(128, 64, F32) for _ in range(2)]
        XTM = AR.take(128, 4 * 512, BF16)
        BTM = AR.take(128, 4 * 512, BF16)
        ZTM = AR.take(128, 2048, BF16)
        SZ = AR.take(128, 2048, F32)
        XDT = AR.take(128, 2048, BF16)
        BDT = AR.take(128, 2048, BF16)
        EXPD = [AR.take(128, 128, F32) for _ in range(2)]
        LT = [AR.take(128, 128, BF16) for _ in range(2)]
        CEX = [AR.take(128, 128, BF16) for _ in range(2)]
        YZ = AR.take(128, 2048, F32)
        YB = AR.take(128, 2048, F32)
        RMSA = AR.take(128, 32, F32)
        RMSB = AR.take(128, 32, F32)
        CONVA = AR.take(128, 512, F32)
        CONVB = AR.take(128, 512, F32)
        CONVT = AR.take(128, 512, F32)
        NT = AR.take(128, 2048, BF16)
        NFM = AR.take(128, 16 * 512, BF16)
        WOBT = AR.take(128, 8 * 512, BF16)
        EOB = [AR.take(128, 512, F32) for _ in range(2)]
        H1, H2 = self.HA, self.HB
        TBF = [self.psum_bf(6), self.psum_bf(7)]

        rt = {}
        g2tile = lambda h: (h // 4)
        tk = self.toks
        t_h0 = P.op("vector", lambda e: e.memset(H1[:], 0.0))
        rt["hstt_last"] = t_h0

        for g in range(NSC):
            T0 = g * 512
            t_xc = {}
            for half in range(4):
                t_xc[half] = P.op("sync", (lambda hh=half, t0=T0: lambda e: e.dma_start(
                    out=XC.rearrange("p (j t) -> p j t", j=6)[:, 0:6, 0:515],
                    in_=self.XBCS.ap()[hh * 6 * 128:(hh + 1) * 6 * 128, t0:t0 + 515]
                        .rearrange("(j p) t -> p j t", p=128)))(),
                    wait=[rt.get("xcread")], dma=True)
                for jl in range(6):
                    j = half * 6 + jl
                    tap = lambda k, jj=jl: XC[:, jj * 520 + k:jj * 520 + k + 512]
                    wc = lambda k, jj=j: self.CONVW[:, jj * 4 + k:jj * 4 + k + 1]
                    if j < 4:
                        dest = XFM[:, j * 512:(j + 1) * 512]
                        dkey = "xfm"
                    elif j < 8:
                        dest = BFM[:, (j - 4) * 512:(j - 3) * 512]
                        dkey = "bfm"
                    else:
                        dest = CFM[:, (j - 8) * 512:(j - 7) * 512]
                        dkey = ("cfm", j - 8)
                    t1 = P.op("vector", (lambda t=tap(0), w=wc(0), jj=j: lambda e: e.tensor_scalar(
                        CONVA, t, w, self.CONVB[:, jj:jj + 1], ALU.mult, ALU.add))(),
                        wait=[t_xc[half], tk["cw"], tk["cb"], rt.get("conva")])
                    t2m = P.op("gpsimd", (lambda t=tap(1), w=wc(1): lambda e: e.tensor_tensor(
                        CONVT, t, w.to_broadcast([128, 512]), ALU.mult))(),
                        wait=[t_xc[half], tk["cw"], rt.get("convt")])
                    t2 = P.op("gpsimd", lambda e: e.tensor_add(CONVB, CONVT, CONVA),
                              wait=[t1, t2m, rt.get("convb")])
                    rt["convt"] = t2
                    t3 = P.op("vector", (lambda t=tap(2), w=wc(2): lambda e: e.scalar_tensor_tensor(
                        CONVA, t, w, CONVB, ALU.mult, ALU.add))(), wait=[t2])
                    t4m = P.op("gpsimd", (lambda t=tap(3), w=wc(3): lambda e: e.tensor_tensor(
                        CONVT, t, w.to_broadcast([128, 512]), ALU.mult))(),
                        wait=[t2, rt.get("convt")])
                    t4 = P.op("gpsimd", lambda e: e.tensor_add(CONVB, CONVT, CONVA),
                              wait=[t3, t4m])
                    rt["convt"] = t4
                    rt["conva"] = t4
                    t5 = P.op("scalar", (lambda d=dest: lambda e: e.activation(
                        d, CONVB, AF.Silu))(),
                        wait=[t4, rt.get(("cread", dkey)), rt.get(("cread2", dkey))])
                    rt["convb"] = t5
                    rt[("csilu", dkey)] = t5
                    rt["xcread"] = t4
            if g == 0:
                self.tap("xfm", XFM, [rt.get(("csilu", "xfm"))])
                self.tap("bfm", BFM, [rt.get(("csilu", "bfm"))])
                self.tap("cfm0", CFM[:, 0:512], [rt.get(("csilu", ("cfm", 0)))])
            bdup_tok = []
            for t in range(8):
                for b2 in range(2):
                    tok = P.op("gpsimd" if (t + b2) % 2 else "scalar",
                               (lambda t_=t, b_=b2: lambda e: e.dma_start(
                                   out=BFMD[b_ * 64:b_ * 64 + 64, t_ * 512:(t_ + 1) * 512],
                                   in_=BFM[(t_ % 2) * 64:(t_ % 2) * 64 + 64,
                                           (t_ // 2) * 512:(t_ // 2) * 512 + 512]))(),
                               wait=[rt.get(("csilu", "bfm")), rt.get("bfmdread")], dma=True)
                    bdup_tok.append(tok)
            rt["bdup"] = bdup_tok

            xtm_tok = {}
            for c4 in range(4):
                for src, dst, name in ((XFM, XTM, "xfm"), (BFM, BTM, "bfm")):
                    evs = []
                    for jj in range(4):
                        sl = (c4 * 8 + (0 if name == "xfm" else 4) + jj) % 2
                        t_tp = P.op("tensor", (lambda s=src, j_=jj, c_=c4, p=TBF[sl]: lambda e: e.transpose(
                            p[0:128, 0:128], s[:, j_ * 512 + c_ * 128:j_ * 512 + c_ * 128 + 128],
                            self.IDB[:]))(),
                            wait=[rt.get(("csilu", name)), tk["idb"], rt.get(("psy", sl)),
                                  rt.get(("tbf", sl))])
                        eng = "vector" if jj % 2 else "scalar"
                        if eng == "vector":
                            fn = (lambda d=dst, j_=jj, c_=c4, p=TBF[sl]: lambda e: e.tensor_copy(
                                d[:, c_ * 512 + j_ * 128:c_ * 512 + j_ * 128 + 128], p[0:128, 0:128]))()
                        else:
                            fn = (lambda d=dst, j_=jj, c_=c4, p=TBF[sl]: lambda e: e.copy(
                                d[:, c_ * 512 + j_ * 128:c_ * 512 + j_ * 128 + 128], p[0:128, 0:128]))()
                        t_ev = P.op(eng, fn, wait=[t_tp, rt.get(("tmread", name, c4))])
                        rt[("tbf", sl)] = t_ev
                        rt[("cread", name)] = t_tp
                        evs.append(t_ev)
                    xtm_tok[(c4, name)] = evs
            for c4 in range(4):
                ci = g * 4 + c4
                cp = ci % 2
                Hrd = H1 if cp == 0 else H2
                Hwr = H2 if cp == 0 else H1
                t_dt = P.op("sync", (lambda c=ci: lambda e: e.dma_start(
                    out=self.DTCH[:], in_=self.DTS.ap()[:, c * 128:(c + 1) * 128]))(),
                    wait=[rt.get("dtread")], dma=True)
                SPT0 = self.SPT[0:NH, 0:128]
                SPT1 = self.SPT[0:NH, 128:256]
                SPT2 = self.SPT[0:NH, 256:384]
                t_s1 = P.op("vector", lambda e: e.tensor_scalar(
                    SPT0, self.DTCH[:], self.DTBC[:, 0:1], None, ALU.add),
                    wait=[t_dt, tk["dtbc"], rt.get("spt0")])
                rt["dtread"] = t_s1
                t_s2 = P.op("scalar", lambda e: e.activation(
                    SPT1, SPT0, AF.Abs), wait=[t_s1, rt.get("spt1")])
                t_s3 = P.op("scalar", lambda e: e.activation(SPT1, SPT1, AF.Exp, scale=-1.0),
                            wait=[t_s2])
                t_s4 = P.op("scalar", lambda e: e.activation(SPT1, SPT1, AF.Ln, bias=1.0),
                            wait=[t_s3])
                t_s5 = P.op("vector", lambda e: e.tensor_scalar(
                    SPT2, SPT0, 0.0, None, ALU.max),
                    wait=[t_s1, rt.get("spt2")])
                rt["spt0"] = t_s5
                t_dtf = P.op("vector", lambda e: e.tensor_add(
                    self.DTF[:], SPT2, SPT1), wait=[t_s5, t_s4, rt.get("dtf_war")])
                rt["spt1"] = t_dtf
                rt["spt2"] = t_dtf
                t_da = P.op("vector", lambda e: e.tensor_scalar(
                    SPT0, self.DTF[:], self.ACOL[:, 0:1], None, ALU.mult),
                    wait=[t_dtf, tk["acol"]])
                t_cum = P.op("vector", (lambda: lambda e: e.tensor_tensor_scan(
                    self.CUM[:], SPT0, self.COLS[0:NH, 2:3].to_broadcast([NH, 128]),
                    0.0, ALU.add, ALU.add))(),
                    wait=[t_da, rt.get("cum_war"), self.t_zero_col])
                t_dec = P.op("scalar", lambda e: e.activation(
                    self.DEC[:], self.CUM[:], AF.Exp, scale=-1.0,
                    bias=self.CUM[:, 127:128]), wait=[t_cum, rt.get("dec_war")])
                t_exc = P.op("scalar", lambda e: e.activation(
                    self.EXPC[:], self.CUM[:], AF.Exp), wait=[t_cum, rt.get("expc")])
                t_b1 = P.op("gpsimd", lambda e: e.dma_start(
                    out=self.BRS.ap()[0:32, :], in_=self.CUM[:]),
                    wait=[t_cum, rt.get("brsread")], dma=True)
                t_b2 = P.op("gpsimd", lambda e: e.dma_start(
                    out=self.BRS.ap()[32:64, :], in_=self.EXPC[:]),
                    wait=[t_exc, rt.get("brsread")], dma=True)
                rt["expc"] = t_b2
                t_br = P.op("gpsimd", lambda e: e.dma_start(
                    out=self.BR[:], in_=self.BRS.ap().rearrange("a b -> (a b)")
                    .rearrange("(o f) -> o f", o=1)),
                    wait=[t_b1, t_b2, rt.get("br_sb_read")], dma=True)
                rt["brsread"] = t_br
                evs_t = []
                for src_t, col, wt in ((self.CUM, 0, [t_cum]), (self.DTF, 32, [t_dtf]),
                                       (self.DEC, 64, [t_dec])):
                    t_stp = P.op("tensor", (lambda s_=src_t: lambda e: e.transpose(
                        self.PS[2][:, 0:32], s_[:], self.IDF[0:32, 0:32]))(),
                        wait=wt + [tk["idf"], rt.get(("psc", 0))])
                    t_tev = P.op("vector", (lambda c_=col: lambda e: e.tensor_copy(
                        self.TCOL[:, c_:c_ + 32], self.PS[2][:, 0:32]))(),
                        wait=[t_stp, rt.get("tcolread")])
                    rt[("psc", 0)] = t_tev
                    evs_t.append(t_tev)
                t_tcol = evs_t[-1]
                rt["cum_war"] = [evs_t[0], t_dec, t_exc, t_b1]
                rt["dtf_war"] = evs_t[1]
                rt["dec_war"] = evs_t[2]
                t_negt = P.op("vector", lambda e: e.tensor_scalar_mul(
                    self.NEGT[:], self.TCOL[:, 0:32], -1.0), wait=[evs_t[0], rt.get("negtread")])
                t_cdmm = P.op("tensor", (lambda: lambda e: e.matmul(
                    self.PS[3][0:64, 0:NH], self.ONE1[:, 0:64],
                    self.BR[0:1, 4096:8192].rearrange("o (h t) -> o h t", h=NH)[:, :, 127:128]
                    .rearrange("o h t -> o (h t)"),
                    start=True, stop=True))(),
                    wait=[t_br, tk["one"], rt.get(("psc", 1))])
                t_cdec = P.op("vector", lambda e: e.tensor_copy(self.CDECB[:], self.PS[3][0:64, 0:NH]),
                              wait=[t_cdmm, rt.get("cdecread")])
                rt[("psc", 1)] = t_cdec
                t_ztm = P.op("sync", (lambda c=ci: lambda e: e.dma_start(
                    out=ZTM, in_=self.ZS.ap()[c * 128:(c + 1) * 128, :]))(),
                    wait=[rt.get("szread_z")], dma=True)
                t_sz = P.op("scalar", lambda e: e.activation(SZ, ZTM, AF.Silu),
                            wait=[t_ztm, rt.get("szread")])
                rt["szread_z"] = t_sz
                t_hbf0 = P.op("gpsimd", (lambda hr=Hrd: lambda e: e.tensor_copy(
                    self.HBF[0:64, :], hr[:]))(),
                    wait=[rt.get("hstt_last"), rt.get("hbfread"), rt.get("hbfdup")])
                t_hbf = P.op("gpsimd", lambda e: e.dma_start(
                    out=self.HBF[64:128, :], in_=self.HBF[0:64, :]),
                    wait=[t_hbf0, rt.get("hbfread")], dma=True)
                rt["hbfdup"] = t_hbf
                xtm4 = XTM[:, c4 * 512:(c4 + 1) * 512].rearrange(
                    "t (g o p) -> t g o p", g=NG, o=1).to_broadcast([128, NG, 4, 64])
                btm4 = BTM[:, c4 * 512:(c4 + 1) * 512].rearrange(
                    "t (g o p) -> t g o p", g=NG, o=1).to_broadcast([128, NG, 4, 64])
                dt4 = self.TCOL[:, 32:64].rearrange("t (g r o) -> t g r o", g=NG, r=4, o=1)\
                    .to_broadcast([128, NG, 4, 64])
                de4 = self.TCOL[:, 64:96].rearrange("t (g r o) -> t g r o", g=NG, r=4, o=1)\
                    .to_broadcast([128, NG, 4, 64])
                xdt4 = XDT.rearrange("t (g r p) -> t g r p", g=NG, r=4)
                bdt4 = BDT.rearrange("t (g r p) -> t g r p", g=NG, r=4)
                t_xdt = P.op("vector", (lambda a=xtm4, b=dt4, o=xdt4: lambda e: e.tensor_tensor(
                    o, a, b, ALU.mult))(),
                    wait=[t_tcol, rt.get("xdtread")] + xtm_tok[(c4, "xfm")])
                dbc4 = self.DBC.rearrange("t (g r o) -> t g r o", g=NG, r=4, o=1)\
                    .to_broadcast([128, NG, 4, 64])
                t_dxt = P.op("gpsimd", (lambda a=xtm4, b=dbc4: lambda e: e.tensor_tensor(
                    DXT.rearrange("t (g r p) -> t g r p", g=NG, r=4), a, b, ALU.mult))(),
                    wait=[tk["dbc"], rt.get("dxtread")] + xtm_tok[(c4, "xfm")])
                t_bdt = P.op("vector", (lambda a=btm4, b=de4, o=bdt4: lambda e: e.tensor_tensor(
                    o, a, b, ALU.mult))(),
                    wait=[t_tcol, rt.get("bdtread")] + xtm_tok[(c4, "bfm")])
                rt["tcolread"] = t_bdt
                rt[("tmread", "xfm", c4)] = t_xdt
                rt[("tmread", "bfm", c4)] = t_bdt

                if ci == 0 and self.taps_on:
                    pass
                if ci == 0:
                    self.tap("cum", self.CUM[:], [t_cum])
                    self.tap("dtf", self.DTF[:], [t_dtf])
                    self.tap("dec", self.DEC[:], [t_dec])
                    self.tap("tcol", self.TCOL[:], [t_tcol])
                    self.tap("negt", self.NEGT[:], [t_negt])
                    self.tap("cdecb", self.CDECB[:], [t_cdec])
                    self.tap("br", self.BR[0:1, :], [t_br])
                    self.tap("xdt", XDT, [t_xdt])
                    self.tap("bdt", BDT, [t_bdt])
                    self.tap("dxt", DXT, [t_dxt])
                    self.tap("xtm0", XTM[:, 0:512], xtm_tok[(0, "xfm")])
                    self.tap("sz", SZ[:, 0:2048], [t_sz])
                t_dblk = {}
                for h in range(NH):
                    blk, sl, bsl = h // 4, h % 2, (h // 4) % 2
                    g2 = g2tile(h)
                    b0 = (h % 2) * 64
                    if h % 4 == 0:
                        P.op("tensor", (lambda b=blk, p=self.PS[bsl]: lambda e: e.matmul(
                            p[:, 0:512], self.ONE1[:], self.BR[0:1, b * 512:(b + 1) * 512],
                            start=True, stop=False))(),
                            wait=[t_br, tk["one"], rt.get(("psd", bsl))], silent=True)
                        t_dblk[blk] = P.op("tensor", (lambda p=self.PS[bsl]: lambda e: e.matmul(
                            p[:, 0:512], self.IDF[:], self.MSK[:],
                            start=False, stop=True))(), wait=[tk["idf"], tk["msk"]])
                    if self.taps_on and ci == 0 and h == 0:
                        t_dcp = P.op("vector", lambda e: e.tensor_copy(CONVT, self.PS[0][:, 0:512]),
                                     wait=[t_dblk[0], rt.get("convt")])
                        t_dtap = self.tap("dpsum", CONVT, [t_dcp])
                        rt["convt"] = t_dtap
                    t_expd = P.op("scalar", (lambda hh=h, b=blk, s_=sl, p=self.PS[bsl]: lambda e: e.activation(
                        EXPD[s_], p[:, (hh % 4) * 128:(hh % 4 + 1) * 128], AF.Exp,
                        bias=self.NEGT[:, hh:hh + 1]))(),
                        wait=[t_dblk[blk], t_negt, rt.get(("expdread", sl))])
                    rt[("psd", bsl)] = t_expd
                    t_cxm = P.op("tensor", (lambda hh=h, s_=sl: lambda e: e.matmul(
                        self.PS[2 + s_][:, 0:128], self.ONE1[:],
                        self.BR[0:1, 4096 + hh * 128:4096 + (hh + 1) * 128],
                        start=True, stop=True))(),
                        wait=[t_br, rt.get(("psc", sl))])
                    if self.taps_on and ci == 0 and h == 0:
                        t_ccp = P.op("vector", lambda e: e.tensor_copy(CONVT[:, 0:128], self.PS[2][:, 0:128]),
                                     wait=[t_cxm, rt.get("convt")])
                        rt["convt"] = self.tap("cxpsum", CONVT[:, 0:128], [t_ccp])
                    t_cxe = P.op("vector", (lambda hh=h, s_=sl, b_=b0, c_=c4: lambda e: e.scalar_tensor_tensor(
                        CEX[s_][b_:b_ + 64, :], self.PS[2 + s_][b_:b_ + 64, 0:128], 1.0,
                        CFM[b_:b_ + 64,
                            (hh // 2) * 512 + c_ * 128:(hh // 2) * 512 + c_ * 128 + 128],
                        ALU.mult, ALU.mult))(),
                        wait=[t_cxm, rt.get(("csilu", ("cfm", h // 2))), rt.get(("cexread", sl))])
                    rt[("psc", sl)] = t_cxe
                    t_gmm = P.op("tensor", (lambda hh=h, s_=sl, gg=g2, c_=c4, b_=b0: lambda e: e.matmul(
                        self.PS[4 + s_][:, 0:128],
                        BFMD[b_:b_ + 64, gg * 512 + c_ * 128:gg * 512 + c_ * 128 + 128],
                        CFM[b_:b_ + 64,
                            (hh // 2) * 512 + c_ * 128:(hh // 2) * 512 + c_ * 128 + 128],
                        start=True, stop=True))(),
                        wait=bdup_tok + [rt.get(("csilu", ("cfm", h // 2))), rt.get(("psg", sl))])
                    t_lev = P.op("vector", (lambda s_=sl: lambda e: e.tensor_tensor(
                        LT[s_], self.PS[4 + s_][:, 0:128], EXPD[s_], ALU.mult))(),
                        wait=[t_gmm, t_expd, rt.get(("ltread", sl))])
                    rt[("psg", sl)] = t_lev
                    rt[("expdread", sl)] = t_lev
                    t_y1 = P.op("tensor", (lambda hh=h, s_=sl: lambda e: e.matmul(
                        self.PS[6 + s_][:, 0:64], LT[s_], XDT[:, hh * 64:(hh + 1) * 64],
                        start=True, stop=True))(),
                        wait=[t_lev, t_xdt, rt.get(("psy", sl))])
                    t_y2 = P.op("tensor", (lambda hh=h, s_=sl, b_=b0: lambda e: e.matmul(
                        self.PS[6 + s_][:, 64:128], CEX[s_][b_:b_ + 64, :],
                        self.HBF[b_:b_ + 64, hh * 64:(hh + 1) * 64],
                        start=True, stop=True))(), wait=[t_cxe, t_hbf])
                    rt[("ltread", sl)] = t_y2
                    rt[("cexread", sl)] = t_y2
                    rt[("cread2", ("cfm", h // 2))] = t_y2
                    rt[("cread2", "bfm")] = t_y2
                    t_smm = P.op("tensor", (lambda hh=h, s_=sl: lambda e: e.matmul(
                        self.PS[6 + s_][0:64, 128:192], BDT[:, hh * 64:(hh + 1) * 64],
                        XDT[:, hh * 64:(hh + 1) * 64], start=True, stop=True))(),
                        wait=[t_bdt])
                    t_hst = P.op("vector", (lambda hh=h, s_=sl, hr=Hrd, hw=Hwr: lambda e: e.scalar_tensor_tensor(
                        hw[:, hh * 64:(hh + 1) * 64], hr[:, hh * 64:(hh + 1) * 64],
                        self.CDECB[:, hh:hh + 1], self.PS[6 + s_][0:64, 128:192],
                        ALU.mult, ALU.add))(),
                        wait=[t_smm, t_cdec, rt.get(("hwar", h)), t_hbf0])
                    rt[("hwar", h)] = t_hst
                    t_ty = P.op("vector", (lambda hh=h, s_=sl: lambda e: e.scalar_tensor_tensor(
                        TY[s_], self.PS[6 + s_][:, 64:128], 1.0,
                        DXT[:, hh * 64:(hh + 1) * 64], ALU.mult, ALU.add))(),
                        wait=[t_y2, t_dxt, rt.get(("tyread", sl))])
                    t_yz = P.op("vector", (lambda hh=h, s_=sl: lambda e: e.tensor_tensor(
                        YZ[:, hh * 64:(hh + 1) * 64], TY[s_], self.PS[6 + s_][:, 0:64],
                        ALU.add))(),
                        wait=[t_y1, t_ty, t_smm, rt.get("yzread")])
                    rt[("tyread", sl)] = t_yz
                    rt[("psy", sl)] = t_yz
                    rt["hstt_last"] = t_hst
                    rt["hbfread"] = t_yz
                    if self.taps_on and ci == 0 and h == 0:
                        tt1 = self.tap("expd0", EXPD[0], [t_expd])
                        rt[("expdread", 0)] = tt1
                        tt2 = self.tap("lt0", LT[0], [t_lev])
                        rt[("ltread", 0)] = tt2
                        tt3 = self.tap("cex0", CEX[0], [t_cxe])
                        rt[("cexread", 0)] = tt3
                        tt4 = self.tap("yz0", YZ[:, 0:64], [t_yz])
                        rt["yzread"] = tt4
                    if self.taps_on and ci == 1 and h == 0:
                        tt5 = self.tap("hbf1", self.HBF[:, 0:2048], [t_hbf])
                        rt["hbfread"] = tt5
                rt["cdecread"] = t_hst
                rt["negtread"] = t_expd
                rt["br_sb_read"] = t_cxm
                rt["xdtread"] = t_smm
                rt["bdtread"] = t_smm
                rt["dxtread"] = t_ty
                rt["bfmdread"] = t_gmm

                t_yb = P.op("gpsimd", lambda e: e.tensor_tensor(YB, YZ, SZ, ALU.mult),
                            wait=[t_yz, t_sz, rt.get("ybread")])
                rt["yzread"] = t_yb
                rt["szread"] = t_yb
                t_sq = P.op("vector", lambda e: e.tensor_tensor(YZ, YB, YB, ALU.mult),
                            wait=[t_yb])
                t_rs = P.op("vector", (lambda: lambda e: e.tensor_reduce(
                    RMSA.rearrange("p (h o) -> p h o", o=1),
                    YZ.rearrange("p (h k) -> p h k", h=NH), AX, ALU.add))(),
                    wait=[t_sq, rt.get("rmsaread")])
                t_r2 = P.op("vector", lambda e: e.tensor_scalar_mul(RMSB, RMSA, 1.0 / 64),
                            wait=[t_rs, rt.get("rmsbread")])
                t_rl = P.op("scalar", lambda e: e.activation(RMSA, RMSB, AF.Ln, bias=self.COLS[:, 9:10]),
                            wait=[t_r2, self.t_eps_ln])
                rt["rmsaread"] = t_rl
                t_re = P.op("scalar", lambda e: e.activation(RMSB, RMSA, AF.Exp, scale=-0.5),
                            wait=[t_rl])
                rt["rmsbread"] = t_re
                t_nt = P.op("vector", (lambda: lambda e: e.tensor_tensor(
                    NT.rearrange("p (h k) -> p h k", h=NH),
                    YB.rearrange("p (h k) -> p h k", h=NH),
                    RMSB.rearrange("p (h o) -> p h o", o=1).to_broadcast([128, NH, 64]),
                    ALU.mult))(), wait=[t_re, t_yb, rt.get("ntread")])
                rt["ybread"] = t_nt
                ntevs = []
                for kk in range(16):
                    sl2 = kk % 2
                    t_tp = P.op("tensor", (lambda k_=kk, p=TBF[sl2]: lambda e: e.transpose(
                        p[0:128, 0:128], NT[:, k_ * 128:(k_ + 1) * 128], self.IDB[:]))(),
                        wait=[t_nt, rt.get(("psy", sl2)), rt.get(("tbf", sl2))])
                    eng = "vector" if kk % 2 else "scalar"
                    if eng == "vector":
                        fn = (lambda k_=kk, c_=c4, p=TBF[sl2]: lambda e: e.tensor_copy(
                            NFM[:, k_ * 512 + c_ * 128:k_ * 512 + c_ * 128 + 128], p[0:128, 0:128]))()
                    else:
                        fn = (lambda k_=kk, c_=c4, p=TBF[sl2]: lambda e: e.copy(
                            NFM[:, k_ * 512 + c_ * 128:k_ * 512 + c_ * 128 + 128], p[0:128, 0:128]))()
                    t_ev = P.op(eng, fn, wait=[t_tp, rt.get("nfmread")])
                    rt[("tbf", sl2)] = t_ev
                    rt[("psy", sl2)] = t_ev
                    ntevs.append(t_ev)
                rt["ntread"] = ntevs[-1]
                rt[("ntevs", c4)] = ntevs
                if ci == 0:
                    ty1 = self.tap("yz", YZ, [t_yz])
                    if ty1 is not None:
                        rt["yzread"] = ty1
                    ty2 = self.tap("nt", NT, [t_nt])
                    if ty2 is not None:
                        rt["ntread"] = ty2

            if self.taps_on and g == 0:
                tnf = self.tap("nfm01", NFM[:, 0:1024], [rt[("ntevs", 3)][-1]])
                if tnf is not None:
                    rt["nfmread"] = tnf
            nfm_done = [t for c_ in range(4) for t in rt[("ntevs", c_)]]
            t_sha_last = None
            for ob in range(4):
                t_mm_l = {}
                for kkb in range(2):
                    t_wo_l = []
                    for kl in range(8):
                        t_wo_l.append(P.op("sync", (lambda k_=kkb * 8 + kl, kl_=kl, o_=ob: lambda e: e.dma_start(
                            out=WOBT[:, kl_ * 512:(kl_ + 1) * 512],
                            in_=self.WOS.ap()[k_ * 128:(k_ + 1) * 128, o_ * 512:(o_ + 1) * 512]))(),
                            wait=[rt.get("wobtread")], dma=True))
                    for tt in range(4):
                        for kl in range(8):
                            kk = kkb * 8 + kl
                            t_mm_l[tt] = P.op("tensor", (lambda k_=kk, kl_=kl, t_=tt, p=self.PS[tt]: lambda e: e.matmul(
                                p[:, 0:512], NFM[:, k_ * 512 + t_ * 128:k_ * 512 + t_ * 128 + 128],
                                WOBT[:, kl_ * 512:(kl_ + 1) * 512],
                                start=(k_ == 0), stop=(k_ == 15)))(),
                                wait=([t_wo_l[kl]] + nfm_done + ([rt.get(("opps", tt))] if kk == 0 else [])),
                                silent=(kl < 7 or tt < 3))
                    rt["wobtread"] = t_mm_l[3]
                rt["nfmread"] = t_mm_l[3]
                for tt in range(4):
                    t_ev = P.op("vector", (lambda p=self.PS[tt], b=EOB[tt % 2]: lambda e: e.tensor_copy(
                        b, p[:, 0:512]))(), wait=[t_mm_l[3], rt.get(("eob", tt % 2))])
                    rt[("opps", tt)] = t_ev
                    if tt < 2:
                        rt[("psd", tt)] = t_ev
                    else:
                        rt[("psc", tt - 2)] = t_ev
                    base = (T0 + tt * 128) * DM + ob * 512
                    sha_fn = self.par_dma(
                        (lambda of=base, b=EOB[tt % 2]: lambda e: e.dma_start(
                            out=bass.AP(self.SHA, of, [[DM, 128], [1, 512]]), in_=b))(),
                        (lambda of=base + SEQ * DM, b=EOB[tt % 2]: lambda e: e.dma_start(
                            out=bass.AP(self.SHA, of, [[DM, 128], [1, 512]]), in_=b))())
                    t_sha = P.op("gpsimd", sha_fn, wait=[t_ev, tk["regs"]], dma=True, manual=True, stream="sha")
                    rt[("eob", tt % 2)] = t_sha
                    t_sha_last = t_sha
            self.flag_write(1 + g, wait=[t_sha_last])

    def ph_ln1(self):
        P, AR = self.P, self.AR
        AR.reset()
        R1 = AR.take(128, 2048, F32)
        SQ = AR.take(128, 2048, F32)
        LNT = AR.take(128, 2048, BF16)
        M1 = self.COLS[:, 4:5]
        Q1 = self.COLS[:, 5:6]
        T2 = self.COLS[:, 6:7]
        VAR = self.COLS[:, 7:8]
        RSD = self.COLS[:, 8:9]
        rt = {}
        for g_rel in range(4):
            t_poll = self.flag_poll_dyn(1, g_rel)
            for tt in range(4):
                off0 = (0 * 4 + g_rel) * 512 * DM + tt * 128 * DM
                off1 = (1 * 4 + g_rel) * 512 * DM + tt * 128 * DM
                t_u = P.op("gpsimd", self.par_dma(
                    (lambda of=off0: lambda e: e.dma_start(
                        out=R1, in_=bass.AP(self.u, of, [[DM, 128], [1, DM]])))(),
                    (lambda of=off1: lambda e: e.dma_start(
                        out=R1, in_=bass.AP(self.u, of, [[DM, 128], [1, DM]])))()),
                    wait=[rt.get("r1read"), self.toks["regs"]], dma=True, manual=True, stream="r1")
                t_a0 = P.op("gpsimd", self.par_dma(
                    (lambda of=off0: lambda e: e.dma_start(
                        out=R1, in_=bass.AP(self.SHA, of, [[DM, 128], [1, DM]]), accum_op=ALU.add))(),
                    (lambda of=off1: lambda e: e.dma_start(
                        out=R1, in_=bass.AP(self.SHA, of, [[DM, 128], [1, DM]]), accum_op=ALU.add))()),
                    wait=[t_u, t_poll], dma=True, manual=True, stream="r1")
                t_a1 = P.op("gpsimd", self.par_dma(
                    (lambda of=off0 + SEQ * DM: lambda e: e.dma_start(
                        out=R1, in_=bass.AP(self.SHA, of, [[DM, 128], [1, DM]]), accum_op=ALU.add))(),
                    (lambda of=off1 + SEQ * DM: lambda e: e.dma_start(
                        out=R1, in_=bass.AP(self.SHA, of, [[DM, 128], [1, DM]]), accum_op=ALU.add))()),
                    wait=[t_a0], dma=True, manual=True, stream="r1")
                t_m = P.op("vector", lambda e: e.tensor_reduce(
                    M1.rearrange("p (a o) -> p a o", o=1),
                    R1.rearrange("p (a k) -> p a k", a=1), AX, ALU.add),
                    wait=[t_a1, rt.get("m1read")])
                t_sq = P.op("scalar", lambda e: e.activation(SQ, R1, AF.Square),
                            wait=[t_a1, rt.get("sqread")])
                t_q = P.op("vector", lambda e: e.tensor_reduce(
                    Q1.rearrange("p (a o) -> p a o", o=1),
                    SQ.rearrange("p (a k) -> p a k", a=1), AX, ALU.add),
                    wait=[t_sq, rt.get("q1read")])
                rt["sqread"] = t_q
                t_msc = P.op("vector", lambda e: e.tensor_scalar_mul(M1, M1, 1.0 / DM),
                             wait=[t_m, rt.get("m1bread")])
                rt["m1read"] = t_msc
                t_mm2 = P.op("vector", lambda e: e.tensor_tensor(
                    T2, M1, M1, ALU.mult), wait=[t_msc, rt.get("t2read")])
                t_var = P.op("vector", lambda e: e.scalar_tensor_tensor(
                    VAR, Q1, 1.0 / DM, T2, ALU.mult, ALU.subtract),
                    wait=[t_mm2, t_q, rt.get("varread")])
                rt["q1read"] = t_var
                rt["t2read"] = t_var
                t_ln = P.op("scalar", lambda e: e.activation(VAR, VAR, AF.Ln, bias=self.COLS[:, 9:10]),
                            wait=[t_var, self.t_eps_ln])
                t_ex = P.op("scalar", lambda e: e.activation(RSD, VAR, AF.Exp, scale=-0.5),
                            wait=[t_ln, rt.get("rsdread")])
                rt["varread"] = t_ex
                t_lnt = P.op("vector", lambda e: e.tensor_scalar(
                    LNT, R1, M1, RSD, ALU.subtract, ALU.mult),
                    wait=[t_msc, t_ex, rt.get("lntread")])
                rt["rsdread"] = t_lnt
                rt["m1bread"] = t_lnt
                rt["r1read"] = t_lnt
                t_st = P.op("sync", (lambda gr=g_rel, t_=tt: lambda e: e.dma_start(
                    out=self.LN1S.ap()[gr * 512 + t_ * 128:gr * 512 + t_ * 128 + 128, :],
                    in_=LNT))(), wait=[t_lnt], dma=True)
                rt["lntread"] = t_st

    def flag_poll_dyn(self, base, g_rel):
        P = self.P

        def fn(e):
            rg = self.regs
            for par_i in (0, 1):
                idx = base + par_i * 4 + g_rel
                src_ap = self.FLG.ap()[1 - par_i:2 - par_i, idx:idx + 1]
                guard = e.If_eq(rg["rpar"], par_i) if par_i == 0 else e.Else()
                with guard:
                    e.reg_mov(rg["rcond"], 1)
                    with e.While(rg["rcond"]):
                        e.dma_start(out=self.FLT[:], in_=src_ap).then_inc(self.psem, 16)
                        e.reg_add(rg["rcnt"], rg["rcnt"], 16)
                        e.wait_ge(self.psem, rg["rcnt"])
                        e.reg_load(rg["rflag"], self.FLT[0:1, 0:1])
                        e.reg_alu(rg["rcond"], rg["rflag"], rg["rnon"], ALU.bitwise_xor)
            return e.nop()
        return P.op("gpsimd", fn, wait=[self.toks["regs"]])

    def ph_f1(self):
        P, AR = self.P, self.AR
        AR.reset()
        LFM = AR.take(128, 64 * 512, BF16)
        WG = [AR.take(128, 16 * 256, BF16) for _ in range(2)]
        WU = [AR.take(128, 16 * 256, BF16) for _ in range(2)]
        SG = AR.take(128, 512, F32)
        HB = AR.take(128, 2048, BF16)
        H2T = AR.take(128, 512, BF16)
        rt = {}
        t_pq = self.flag_poll(0)
        lfm_tok = {}
        for k in range(16):
            for tb in range(4):
                lfm_tok[(k, tb)] = P.op("scalar", (lambda k_=k, t_=tb: lambda e: e.dma_start_transpose(
                    LFM[:, (k_ * 4 + t_) * 512:(k_ * 4 + t_ + 1) * 512],
                    self.LN1S.ap()[t_ * 512:(t_ + 1) * 512, k_ * 128:(k_ + 1) * 128]))(),
                    dma=True, stream="xpose")
        lfm = lambda k, tb: LFM[:, (k * 4 + tb) * 512:(k * 4 + tb + 1) * 512]
        if self.taps_on:
            for kq in range(4):
                self.tap(f"lfm{kq}", LFM[:, kq * 8192:(kq + 1) * 8192],
                         [lfm_tok[(k, tb)] for k in range(kq * 4, kq * 4 + 4) for tb in range(4)])
        for jb in range(32):
            u = jb % 2
            t_wg_l, t_wu_l = [], []
            for k in range(16):
                t_wg_l.append(P.op("scalar", (lambda j_=jb, k_=k, b=WG[u]: lambda e: e.dma_start_transpose(
                    b[:, k_ * 256:(k_ + 1) * 256],
                    self.WQGU.ap()[j_ * 256:(j_ + 1) * 256, k_ * 128:(k_ + 1) * 128]))(),
                    wait=[t_pq, rt.get(("wgread", u))], dma=True, stream="xpose"))
                t_wu_l.append(P.op("scalar", (lambda j_=jb, k_=k, b=WU[u]: lambda e: e.dma_start_transpose(
                    b[:, k_ * 256:(k_ + 1) * 256],
                    self.WQGU.ap()[HF + j_ * 256:HF + (j_ + 1) * 256, k_ * 128:(k_ + 1) * 128]))(),
                    wait=[t_pq, rt.get(("wuread", u))], dma=True, stream="xpose"))
            for jj in range(2):
                j = jb * 2 + jj
                for tb in range(4):
                    pg = self.PS[tb % 2]
                    pu = self.PS[2 + tb % 2]
                    t_g = t_u2 = None
                    for k in range(16):
                        t_g = P.op("tensor", (lambda k_=k, j_=jj, t_=tb, p=pg, b=WG[u]: lambda e: e.matmul(
                            p[:, 0:512], b[:, k_ * 256 + j_ * 128:k_ * 256 + (j_ + 1) * 128],
                            lfm(k_, t_), start=(k_ == 0), stop=(k_ == 15)))(),
                            wait=([t_wg_l[k], lfm_tok[(k, tb)]] + ([rt.get(("pg", tb % 2))] if k == 0 else [])),
                            silent=(k < 15))
                    for k in range(16):
                        t_u2 = P.op("tensor", (lambda k_=k, j_=jj, t_=tb, p=pu, b=WU[u]: lambda e: e.matmul(
                            p[:, 0:512], b[:, k_ * 256 + j_ * 128:k_ * 256 + (j_ + 1) * 128],
                            lfm(k_, t_), start=(k_ == 0), stop=(k_ == 15)))(),
                            wait=([t_wu_l[k], lfm_tok[(k, tb)]] + ([rt.get(("pu", tb % 2))] if k == 0 else [])),
                            silent=(k < 15))
                    rt[("wgread", u)] = t_g
                    rt[("wuread", u)] = t_u2
                    t_sg = P.op("scalar", (lambda p=pg: lambda e: e.activation(SG, p[:, 0:512], AF.Silu))(),
                                wait=[t_g, rt.get("sgread")])
                    rt[("pg", tb % 2)] = t_sg
                    if self.taps_on and j == 4 and tb == 0:
                        tq = P.op("vector", (lambda p=pg: lambda e: e.tensor_copy(HB[:, 0:512].bitcast(F32)[:, 0:256] if False else SG, p[:, 0:512]))(), wait=[t_g], silent=True)
                        ttap = self.tap("gate40", SG, [t_sg])
                        if ttap is not None:
                            rt["sgread"] = ttap
                    t_h = P.op("vector", (lambda p=pu, t_=tb: lambda e: e.tensor_tensor(
                        HB[:, t_ * 512:(t_ + 1) * 512], SG, p[:, 0:512], ALU.mult))(),
                        wait=[t_sg, t_u2, rt.get(("hbread", tb))])
                    rt[("pu", tb % 2)] = t_h
                    rt["sgread"] = t_h
                    t_h2 = P.op("scalar", (lambda t_=tb: lambda e: e.activation(
                        H2T, HB[:, t_ * 512:(t_ + 1) * 512], AF.Square))(),
                        wait=[t_h, rt.get("h2read")])
                    t_sq = P.op("tensor", (lambda t_=tb, j_=j: lambda e: e.matmul(
                        self.PS[4 + t_][0:1, 0:512], self.ONEB[:], H2T,
                        start=(j_ == 0), stop=(j_ == 63)))(),
                        wait=[t_h2, self.toks["oneb"]])
                    rt["h2read"] = t_sq
                t_hd = P.op("sync", (lambda j_=j: lambda e: e.dma_start(
                    out=self.HS.ap()[j_ * 128:(j_ + 1) * 128, :], in_=HB))(),
                    wait=[rt.get(("pu", 0)), rt.get(("pu", 1))], dma=True)
                for tb in range(4):
                    rt[("hbread", tb)] = t_hd
        t_row = []
        for tb in range(4):
            t_ev = P.op("vector", (lambda t_=tb: lambda e: e.tensor_copy(
                self.BR[0:1, t_ * 512:(t_ + 1) * 512], self.PS[4 + t_][0:1, 0:512]))(),
                wait=[rt["h2read"], rt.get("brrow")])
            t_row.append(t_ev)
        t_r1 = P.op("vector", lambda e: e.tensor_scalar_mul(
            self.BR[0:1, 2048:4096], self.BR[0:1, 0:2048], 1.0 / HF), wait=t_row)
        t_r2 = P.op("scalar", lambda e: e.activation(
            self.BR[0:1, 0:2048], self.BR[0:1, 2048:4096], AF.Ln,
            bias=self.COLS[0:1, 10:11]), wait=[t_r1, self.t_eps_ffn])
        t_r3 = P.op("scalar", lambda e: e.activation(
            self.BR[0:1, 2048:4096], self.BR[0:1, 0:2048], AF.Exp, scale=-0.5), wait=[t_r2])
        t_rd = P.op("gpsimd", lambda e: e.dma_start(out=self.RSS.ap(),
                    in_=self.BR[0:1, 2048:4096]), wait=[t_r3], dma=True)
        self.t_rst2 = P.op("sync", lambda e: e.dma_start(
            out=self.RST2[:], in_=self.RSS.ap().rearrange("o (c p) -> (o p) c", p=128),
            allow_slow_non_contiguous=True),
            wait=[t_rd], dma=True)

    def ph_f2(self):
        P, AR = self.P, self.AR
        AR.reset()
        HSL = AR.take(128, 64 * 512, BF16)
        WPT = [AR.take(128, 1024, BF16) for _ in range(2)]
        R2 = [AR.take(128, 2048, F32) for _ in range(4)]
        LNTF = AR.take(128, 4 * 2048, BF16)
        OUT = AR.take(128, 2048, F32)
        M1 = self.COLS[:, 4:5]
        Q1 = self.COLS[:, 5:6]
        T2 = self.COLS[:, 6:7]
        VAR = self.COLS[:, 7:8]
        RSD = self.COLS[:, 8:9]
        SQ2 = OUT
        rt = {}
        for tblk in range(4):
            hs_tok = {}
            for j in range(64):
                hs_tok[j] = P.op("sync", (lambda j_=j, t0=tblk: lambda e: e.dma_start(
                    out=HSL[:, j_ * 512:(j_ + 1) * 512],
                    in_=self.HS.ap()[j_ * 128:(j_ + 1) * 128, t0 * 512:(t0 + 1) * 512]))(),
                    wait=[rt.get(("hslread", j))], dma=True)
            for dh in range(2):
                t_mm_last = {}
                for j in range(64):
                    u = j % 2
                    t_wp = P.op("scalar", (lambda j_=j, d_=dh, b=WPT[u]: lambda e: e.dma_start_transpose(
                        b, self.WQP.ap()[d_ * 1024:(d_ + 1) * 1024, j_ * 128:(j_ + 1) * 128]))(),
                        wait=[rt.get(("wptread", u))], dma=True, stream="xpose")
                    for tt in range(4):
                        for ob in range(2):
                            pi = tt * 2 + ob
                            t_mm = P.op("tensor", (lambda j_=j, t_=tt, o_=ob, p=self.PS[pi], b=WPT[u]: lambda e: e.matmul(
                                p[:, 0:512], HSL[:, j_ * 512 + t_ * 128:j_ * 512 + t_ * 128 + 128],
                                b[:, o_ * 512:(o_ + 1) * 512],
                                start=(j_ == 0), stop=(j_ == 63)))(),
                                wait=([t_wp, hs_tok[j]] + ([rt.get(("pf", pi))] if j == 0 else [])),
                                silent=(j < 63))
                            t_mm_last[pi] = t_mm
                    rt[("wptread", u)] = t_mm_last[7]
                for j in range(64):
                    rt[("hslread", j)] = t_mm_last[7]
                for tt in range(4):
                    if dh == 0:
                        self._lf_tok = getattr(self, "_lf_tok", {})
                        self._lf_tok[tt] = P.op("sync", (lambda t0=tblk, t_=tt: lambda e: e.dma_start(
                            out=LNTF[:, t_ * 2048:(t_ + 1) * 2048],
                            in_=self.LN1S.ap()[t0 * 512 + t_ * 128:t0 * 512 + t_ * 128 + 128, :]))(),
                            wait=[rt.get(("lntfread", tt))], dma=True)
                    for ob in range(2):
                        pi = tt * 2 + ob
                        col = dh * 1024 + ob * 512
                        t_ev = P.op("vector", (lambda p=self.PS[pi], r=R2[tt], c=col, t0=tblk, t_=tt: lambda e: e.scalar_tensor_tensor(
                            r[:, c:c + 512], p[:, 0:512], self.RST2[:, t0 * 4 + t_:t0 * 4 + t_ + 1],
                            LNTF[:, t_ * 2048 + c:t_ * 2048 + c + 512], ALU.mult, ALU.add))(),
                            wait=[t_mm_last[pi], self.t_rst2, self._lf_tok[tt],
                                  rt.get(("r2read", tt))])
                        rt[("pf", pi)] = t_ev
                        rt[("evtok", tt, dh, ob)] = t_ev
            if self.taps_on and tblk == 0:
                tr2 = self.tap("r2t0", R2[0], [rt[("evtok", 0, d_, o_)] for d_ in range(2) for o_ in range(2)])
                if tr2 is not None:
                    rt[("r2read", 0)] = [tr2, rt.get(("r2read", 0))]
            for tt in range(4):
                evs = [rt[("evtok", tt, d_, o_)] for d_ in range(2) for o_ in range(2)]
                t_m = P.op("vector", (lambda r=R2[tt]: lambda e: e.tensor_reduce(
                    M1.rearrange("p (a o) -> p a o", o=1),
                    r.rearrange("p (a k) -> p a k", a=1), AX, ALU.add))(),
                    wait=evs + [rt.get("m1read")])
                t_sq = P.op("scalar", (lambda r=R2[tt]: lambda e: e.activation(SQ2, r, AF.Square))(),
                            wait=evs + [rt.get("sqread"), rt.get("outread")])
                t_q = P.op("vector", lambda e: e.tensor_reduce(
                    Q1.rearrange("p (a o) -> p a o", o=1),
                    SQ2.rearrange("p (a k) -> p a k", a=1), AX, ALU.add),
                    wait=[t_sq, rt.get("q1read")])
                rt["sqread"] = t_q
                t_msc = P.op("vector", lambda e: e.tensor_scalar_mul(M1, M1, 1.0 / DM),
                             wait=[t_m, rt.get("m1bread")])
                rt["m1read"] = t_msc
                t_mm2 = P.op("vector", lambda e: e.tensor_tensor(
                    T2, M1, M1, ALU.mult), wait=[t_msc, rt.get("t2read")])
                t_var = P.op("vector", lambda e: e.scalar_tensor_tensor(
                    VAR, Q1, 1.0 / DM, T2, ALU.mult, ALU.subtract),
                    wait=[t_mm2, t_q, rt.get("varread")])
                rt["q1read"] = t_var
                rt["t2read"] = t_var
                t_ln = P.op("scalar", lambda e: e.activation(VAR, VAR, AF.Ln, bias=self.COLS[:, 9:10]),
                            wait=[t_var, self.t_eps_ln])
                t_ex = P.op("scalar", lambda e: e.activation(RSD, VAR, AF.Exp, scale=-0.5),
                            wait=[t_ln, rt.get("rsdread")])
                rt["varread"] = t_ex
                t_out = P.op("vector", (lambda r=R2[tt]: lambda e: e.tensor_scalar(
                    OUT, r, M1, RSD, ALU.subtract, ALU.mult))(),
                    wait=[t_msc, t_ex, rt.get("outread")])
                rt["rsdread"] = t_out
                rt["m1bread"] = t_out
                rt[("r2read", tt)] = t_out
                t_st = P.op("sync", (lambda t0=tblk, t_=tt: lambda e: e.dma_start(
                    out=self.outt.ap()[t0 * 512 + t_ * 128:t0 * 512 + t_ * 128 + 128, :],
                    in_=OUT))(), wait=[t_out], dma=True)
                rt["outread"] = t_st
                rt[("lntfread", tt)] = t_st

    def ph_dumps(self):
        P = self.P
        for name in self.dumps:
            src = getattr(self, name)
            dst = self.dump_t[name]
            sap, dap = src.ap(), dst.ap()
            if len(sap.shape) == 3:
                sap = sap.rearrange("a b c -> (a b) c")
                dap = dap.rearrange("a b c -> (a b) c")
            rows = sap.shape[0]
            step = 512
            for r0 in range(0, rows, step):
                r1 = min(r0 + step, rows)
                P.op("sync", (lambda s=sap[r0:r1, :], d=dap[r0:r1, :]: lambda e: e.dma_start(
                    out=d, in_=s))(), dma=True)


def build_nc(dumps=(), stop_after=None, taps_on=False):
    kb = KB(dumps=dumps, taps_on=taps_on)
    nc = kb.build(stop_after=stop_after)
    return nc, kb


def kernel(L0_attn_u, L0_attn_W_in, L0_attn_conv_w, L0_attn_conv_b,
           L0_attn_dt_bias, L0_attn_A_log, L0_attn_D, L0_attn_norm_weight,
           L0_attn_W_out, L0_ffn_w_gate_up, L0_ffn_w_proj,
           _dumps=(), _stop_after=None):
    nonce = int(np.random.randint(1, 2 ** 31 - 1))
    maps = make_in_maps(L0_attn_u, L0_attn_W_in, L0_attn_conv_w, L0_attn_conv_b,
                        L0_attn_dt_bias, L0_attn_A_log, L0_attn_D,
                        L0_attn_norm_weight, L0_attn_W_out, L0_ffn_w_gate_up,
                        L0_ffn_w_proj, nonce)
    nc, kb = build_nc(dumps=_dumps, stop_after=_stop_after)
    res = run_bass_kernel_spmd(nc, maps, core_ids=list(range(8)))
    out = np.zeros((4, 4096, 2048), np.float32)
    for c in range(8):
        s, par = c // 2, c % 2
        out[s, par * 2048:(par + 1) * 2048, :] = np.asarray(res.results[c]["outp"])
    if _dumps or _stop_after:
        return out, res
    return out


# version 03
# speedup vs baseline: 1.0003x; 1.0003x over previous
import contextlib
import zlib
import numpy as np

import concourse.bass as bass
import concourse.mybir as mybir
from concourse.bass_utils import run_bass_kernel_spmd

F32 = mybir.dt.float32
BF16 = mybir.dt.bfloat16
I32 = mybir.dt.int32
AF = mybir.ActivationFunctionType
ALU = mybir.AluOpType
AX = mybir.AxisListType.X

SEQ = 4096
DM = 2048
NH = 32
DST = 64
XCH = 512
NG = 8
CCH = 2048
CONVC = XCH + XCH + CCH
NCT = CONVC // 128
DIN = 2048
CH = 128
NCHUNK = SEQ // CH
SC = 4
NSC = NCHUNK // SC
TOKH = 2048
HF = 8192
GU = 2 * HF
EPS_LN = 1e-5
EPS_RMS = 1e-5
EPS_FFN = float(np.finfo(np.float32).eps)
NEG = -3.0e38

ENGINES = ("sync", "vector", "scalar", "tensor", "gpsimd")
DMA_ENG = {"sync", "gpsimd", "scalar"}


class Prog:

    NCH = 4

    def __init__(self):
        self.items = []
        self.cnt = {}
        for e in ENGINES:
            self.cnt[e] = 0
        for e in ("sync", "scalar", "gpsimd"):
            for i in range(self.NCH):
                self.cnt[f"{e}_c{i}"] = 0
        self.last_chan = {}
        self._rr = {e: 0 for e in ("sync", "scalar", "gpsimd")}

    @staticmethod
    def _flat(wait):
        out = []
        for t in wait:
            if t is None:
                continue
            if isinstance(t, list):
                out.extend(x for x in t if x is not None)
            else:
                out.append(t)
        return tuple(out)

    def op(self, eng, fn, wait=(), dma=False, silent=False, stream=None, manual=False):
        if dma:
            assert not silent
            if stream is None:
                ch = self._rr[eng]
                self._rr[eng] = (ch + 1) % self.NCH
            else:
                ch = zlib.crc32(repr(stream).encode()) % self.NCH
            semname = f"{eng}_c{ch}"
            self.cnt[semname] += 16
            tok = (semname, self.cnt[semname])
            prev = self.last_chan.get(semname)
            self.last_chan[semname] = tok
            self.items.append((eng, fn, self._flat(list(wait) + [prev]), tok, 16,
                               semname, manual))
            return tok
        if silent:
            self.items.append((eng, fn, self._flat(wait), None, 0, eng, False))
            return None
        self.cnt[eng] += 1
        tok = (eng, self.cnt[eng])
        self.items.append((eng, fn, self._flat(wait), tok, 1, eng, False))
        return tok

    def barrier(self):
        toks = [(s, c) for s, c in self.cnt.items() if c > 0]
        new = []
        for e in ENGINES:
            new.append(self.op(e, lambda eng: eng.nop(), wait=toks))
        return new

    def replay(self, eng_name, eng, sems):
        hw = {}
        for item in self.items:
            item_eng, fn, waits, tok, inc, semname, manual = item
            if item_eng != eng_name:
                continue
            for (we, wv) in waits:
                if hw.get(we, 0) < wv:
                    eng.wait_ge(sems[we], wv)
                    hw[we] = wv
            if manual:
                fn(eng, sems[semname])
            else:
                inst = fn(eng)
                if inst is not None and inc:
                    inst.then_inc(sems[semname], inc)


def ceil_div(a, b):
    return -(-a // b)



def make_in_maps(L0_attn_u, L0_attn_W_in, L0_attn_conv_w, L0_attn_conv_b,
                 L0_attn_dt_bias, L0_attn_A_log, L0_attn_D, L0_attn_norm_weight,
                 L0_attn_W_out, L0_ffn_w_gate_up, L0_ffn_w_proj, nonce):
    c32 = lambda x: np.ascontiguousarray(x, dtype=np.float32)
    u = c32(L0_attn_u)
    W_in = c32(L0_attn_W_in)
    conv_w = c32(L0_attn_conv_w)
    conv_b = c32(L0_attn_conv_b)
    dt_bias = c32(L0_attn_dt_bias)
    A_log = c32(L0_attn_A_log)
    Dsk = c32(L0_attn_D)
    gamma = c32(L0_attn_norm_weight)
    W_out = c32(L0_attn_W_out)
    Wgu = c32(L0_ffn_w_gate_up)
    Wp = c32(L0_ffn_w_proj)

    idb = np.eye(128, dtype=np.float32).astype(np.dtype("bfloat16") if False else np.float32)
    idf = np.eye(128, dtype=np.float32)
    j = np.arange(128)
    mask1 = np.where(j[:, None] <= j[None, :], 0.0, NEG).astype(np.float32)
    mask4 = np.tile(mask1, (1, 4))

    maps = []
    for core in range(8):
        s, par = core // 2, core % 2
        zc = par * 2048
        xc = par * XCH
        bc = 1024 + par * XCH
        cc = 2048 + par * CCH
        dtc = 4096 + 6144 + par * NH
        Wz = W_in[:, 4096 + 0:0][:, :0]
        Wz = W_in[:, zc:zc + DIN]
        Wxbc = np.concatenate([
            W_in[:, 4096 + xc:4096 + xc + XCH],
            W_in[:, 4096 + bc:4096 + bc + XCH],
            W_in[:, 4096 + cc:4096 + cc + CCH],
            W_in[:, dtc:dtc + NH],
        ], axis=1)
        rows = np.concatenate([
            np.arange(xc, xc + XCH),
            np.arange(bc, bc + XCH),
            np.arange(cc, cc + CCH),
        ])
        cw = conv_w[rows].reshape(NCT, 128, 4).transpose(1, 0, 2).reshape(128, NCT * 4)
        cb = conv_b[rows].reshape(NCT, 128).T.copy()
        gam = gamma[par * DIN:(par + 1) * DIN].reshape(16, 128).T.copy()
        maps.append({
            "u": np.ascontiguousarray(u[s]),
            "wz": np.ascontiguousarray(Wz),
            "wxbc": np.ascontiguousarray(Wxbc),
            "wout": np.ascontiguousarray(W_out[par * DIN:(par + 1) * DIN]),
            "wgu_sh": np.ascontiguousarray(Wgu[par * HF:(par + 1) * HF]),
            "wp_sh": np.ascontiguousarray(Wp[par * 1024:(par + 1) * 1024]),
            "convw": np.ascontiguousarray(cw),
            "convb": np.ascontiguousarray(cb),
            "gam": np.ascontiguousarray(gam),
            "dtb": np.ascontiguousarray(dt_bias[par * NH:(par + 1) * NH].reshape(NH, 1)),
            "alog": np.ascontiguousarray(A_log[par * NH:(par + 1) * NH].reshape(NH, 1)),
            "dsk": np.ascontiguousarray(Dsk[par * NH:(par + 1) * NH].reshape(NH, 1)),
            "idf": idf,
            "mask4": mask4,
            "parity": np.array([[par]], np.int32),
            "nonce": np.array([[nonce]], np.int32),
        })
    return maps


class Arena:

    def __init__(self, handle, total):
        self.h = handle
        self.total = total
        self.off = 0

    def reset(self):
        self.off = 0

    def take(self, rows, cols, dtype):
        unit = 2 if dtype == F32 else 1
        n = cols * unit
        n = (n + 63) & ~63
        assert self.off + n <= self.total, f"arena overflow {self.off}+{n}>{self.total}"
        ap = self.h[:, self.off:self.off + n]
        self.off += n
        if dtype == F32:
            ap = ap.bitcast(F32)
        ap = ap[0:rows, 0:cols]
        return ap


class KB:
    def __init__(self, dumps=(), taps_on=False):
        self.taps_on = taps_on
        self.tap_map = {}
        self.tap_row = 0
        self.tap_row_b = 0
        self._zev = {}
        self._xev = {}
        self.dumps = set(dumps)
        self.nc = bass.Bass("TRN2", target_bir_lowering=False)
        self.P = Prog()
        self.es = contextlib.ExitStack()

    def declare(self):
        nc = self.nc
        d = nc.dram_tensor
        self.u = d("u", [SEQ, DM], F32, kind="ExternalInput")
        self.wz = d("wz", [DM, DIN], F32, kind="ExternalInput")
        self.wxbc = d("wxbc", [DM, CONVC + NH], F32, kind="ExternalInput")
        self.wout = d("wout", [DIN, DM], F32, kind="ExternalInput")
        self.wgu_sh = d("wgu_sh", [HF, DM], F32, kind="ExternalInput")
        self.wp_sh = d("wp_sh", [1024, HF], F32, kind="ExternalInput")
        self.convw = d("convw", [128, NCT * 4], F32, kind="ExternalInput")
        self.convb = d("convb", [128, NCT], F32, kind="ExternalInput")
        self.gam = d("gam", [128, 16], F32, kind="ExternalInput")
        self.dtb = d("dtb", [NH, 1], F32, kind="ExternalInput")
        self.alog = d("alog", [NH, 1], F32, kind="ExternalInput")
        self.dsk = d("dsk", [NH, 1], F32, kind="ExternalInput")
        self.idf = d("idf", [128, 128], F32, kind="ExternalInput")
        self.mask4 = d("mask4", [128, 512], F32, kind="ExternalInput")
        self.parity = d("parity", [1, 1], I32, kind="ExternalInput")
        self.noncet = d("nonce", [1, 1], I32, kind="ExternalInput")
        self.outt = d("outp", [TOKH, DM], F32, kind="ExternalOutput")

        self.XBCS = d("XBCS", [CONVC, SEQ + 3], BF16)
        self.ZS = d("ZS", [SEQ, DIN], BF16)
        self.DTS = d("DTS", [NH, SEQ], F32)
        self.HS = d("HS", [HF, TOKH], BF16)
        self.LN1S = d("LN1S", [TOKH, DM], BF16)
        self.WOS = d("WOS", [DIN, DM], BF16)
        self.BRS = d("BRS", [64, 128], F32)
        self.RSS = d("RSS", [1, TOKH], F32)
        self.WQGU = d("WQGU", [GU, DM], BF16, addr_space="Shared")
        self.WQP = d("WQP", [DM, HF], BF16, addr_space="Shared")
        self.SHA = d("SHA", [2, SEQ, DM], F32, addr_space="Shared")
        self.FLG = d("FLG", [2, 32], I32, addr_space="Shared")

        self.DBG = d("DBG", [4096, 8192], F32)
        self.DBGB = d("DBGB", [2048, 8192], BF16)
        self.dump_t = {}
        for name in self.dumps:
            src = getattr(self, name)
            self.dump_t[name] = d("dump_" + name, list(src.shape), src.dtype
                                  if hasattr(src, "dtype") else F32, kind="ExternalOutput")

    def alloc_sbuf(self):
        nc, es = self.nc, self.es
        ARN_COLS = 131072
        self.AR = Arena(es.enter_context(nc.sbuf_tensor("ARN", [128, 65536], BF16)), 65536)
        t = lambda n, s, dt: es.enter_context(nc.sbuf_tensor(n, s, dt))
        self.IDB = t("IDB", [128, 128], BF16)
        self.IDF = t("IDFs", [128, 128], F32)
        self.MSK = t("MSK", [128, 512], F32)
        self.ONE1 = t("ONE1", [1, 128], F32)
        self.ONEB = t("ONEB", [128, 1], BF16)
        self.CONVW = t("CONVWs", [128, NCT * 4], F32)
        self.CONVB = t("CONVBs", [128, NCT], F32)
        self.GCOL = t("GCOL", [128, 16], F32)
        self.DTBC = t("DTBC", [NH, 1], F32)
        self.ACOL = t("ACOL", [NH, 1], F32)
        self.DBC = t("DBC", [128, NH], F32)
        self.DSKR = t("DSKR", [1, NH], F32)
        self.PARN = t("PARN", [1, 4], I32)
        self.CUM = t("CUM", [NH, 128], F32)
        self.DTF = t("DTF", [NH, 128], F32)
        self.DEC = t("DEC", [NH, 128], F32)
        self.EXPC = t("EXPC", [NH, 128], F32)
        self.TCOL = t("TCOL", [128, 96], F32)
        self.NEGT = t("NEGT", [128, 32], F32)
        self.CDECB = t("CDECB", [64, NH], F32)
        self.SPT = t("SPT", [NH, 384], F32)
        self.BR = t("BR", [1, 8192], F32)
        self.HA = t("HA", [64, NH * 64], F32)
        self.HB = t("HB", [64, NH * 64], F32)
        self.HBF = t("HBF", [128, NH * 64], BF16)
        self.COLS = t("COLS", [128, 16], F32)
        self.RST2 = t("RST2", [128, 16], F32)
        self.FLT = t("FLT", [1, 1], I32)
        self.DTCH = t("DTCH", [NH, 128], F32)
        self.PS = [es.enter_context(nc.psum_tensor(f"PS{i}", [128, 512], F32))
                   for i in range(8)]

    def psum_bf(self, i):
        return self.PS[i].ap().bitcast(BF16)

    def build(self, stop_after=None):
        self.declare()
        with self.es:
            self.alloc_sbuf()
            P = self.P
            self.tok_regs = None
            phases = ["setup", "quant", "inproj", "wosprep", "ssd", "ln1", "f1", "f2"]
            for ph in phases:
                getattr(self, "ph_" + ph)()
                P.barrier()
                if stop_after == ph:
                    break
            self.ph_dumps()
            P.barrier()
            sems = {}
            for name in self.P.cnt.keys():
                sems[name] = self.es.enter_context(self.nc.semaphore("sem_" + name))
            self.sems = sems
            self.psem = self.es.enter_context(self.nc.semaphore("psem"))
            block = self.es.enter_context(self.nc.Block())
            P = self.P

            @block.sync
            def _(sync):
                P.replay("sync", sync, sems)

            @block.vector
            def _(vector):
                P.replay("vector", vector, sems)

            @block.scalar
            def _(scalar):
                P.replay("scalar", scalar, sems)

            @block.tensor
            def _(tensor):
                P.replay("tensor", tensor, sems)

            @block.gpsimd
            def _(gpsimd):
                self._greg_ctx = contextlib.ExitStack()
                with self._greg_ctx:
                    regs = {}
                    for rn in ["rpar", "roth", "rnon", "rflag", "rcond", "rcnt",
                               "ra", "rb"]:
                        regs[rn] = self._greg_ctx.enter_context(gpsimd.register(rn))
                    self.regs = regs
                    P.replay("gpsimd", gpsimd, sems)

        return self.nc

    def ph_setup(self):
        P = self.P
        s = lambda fn: P.op("sync", fn, dma=True)
        t_idf = s(lambda e: e.dma_start(out=self.IDF[:], in_=self.idf.ap()))
        t_msk = s(lambda e: e.dma_start(out=self.MSK[:], in_=self.mask4.ap()))
        t_cw = s(lambda e: e.dma_start(out=self.CONVW[:], in_=self.convw.ap()))
        t_cb = s(lambda e: e.dma_start(out=self.CONVB[:], in_=self.convb.ap()))
        t_g = s(lambda e: e.dma_start(out=self.GCOL[:], in_=self.gam.ap()))
        t_dtb = s(lambda e: e.dma_start(out=self.DTBC[:], in_=self.dtb.ap()))
        t_al = s(lambda e: e.dma_start(out=self.COLS[0:NH, 0:1], in_=self.alog.ap()))
        t_dr = s(lambda e: e.dma_start(out=self.DSKR[:], in_=self.dsk.ap().rearrange("h o -> o h").rearrange("o h -> (o h)").rearrange("(o h) -> o h", o=1)))
        t_par = s(lambda e: e.dma_start(out=self.PARN[0:1, 0:1], in_=self.parity.ap()))
        t_non = s(lambda e: e.dma_start(out=self.PARN[0:1, 1:2], in_=self.noncet.ap()))
        self.t_msk = t_msk
        t_one = P.op("vector", lambda e: e.memset(self.ONE1[:], 1.0))
        self.t_zero_col = P.op("vector", lambda e: e.memset(self.COLS[:, 2:3], 0.0))
        self.t_eps_ln = P.op("vector", lambda e: e.memset(self.COLS[:, 9:10], EPS_LN))
        self.t_eps_ffn = P.op("vector", lambda e: e.memset(self.COLS[:, 10:11], EPS_FFN))
        t_oneb = P.op("vector", lambda e: e.memset(self.ONEB[:], 1.0))
        t_idb = P.op("vector", lambda e: e.tensor_copy(self.IDB[:], self.IDF[:]), wait=[t_idf])
        t_e = P.op("scalar", lambda e: e.activation(self.COLS[0:NH, 1:2], self.COLS[0:NH, 0:1], AF.Exp), wait=[t_al])
        t_acol = P.op("vector", lambda e: e.tensor_scalar_mul(self.ACOL[:], self.COLS[0:NH, 1:2], -1.0), wait=[t_e])
        t_dmm = P.op("tensor", lambda e: e.matmul(self.PS[0][:, 0:NH], self.ONE1[:], self.DSKR[:],
                                                  start=True, stop=True), wait=[t_one, t_dr])
        t_dbc = P.op("vector", lambda e: e.tensor_copy(self.DBC[:], self.PS[0][:, 0:NH]), wait=[t_dmm])
        def setup_regs(e):
            r = self.regs
            e.reg_load(r["rpar"], self.PARN[0:1, 0:1])
            e.reg_load(r["rnon"], self.PARN[0:1, 1:2])
            e.reg_mov(r["rcnt"], 0)
            return e.reg_alu(r["roth"], 1, r["rpar"], ALU.subtract)
        t_regs = P.op("gpsimd", setup_regs, wait=[t_par, t_non])
        self.toks = dict(idb=t_idb, acol=t_acol, dbc=t_dbc, regs=t_regs, cw=t_cw,
                         cb=t_cb, gcol=t_g, dtbc=t_dtb, oneb=t_oneb, one=t_one,
                         idf=t_idf, msk=t_msk)

    def tap(self, name, ap, wait, cast=False):
        if not getattr(self, "taps_on", False):
            return None
        isbf = ap.dtype == BF16
        dst = self.DBGB if isbf else self.DBG
        rowkey = "tap_row_b" if isbf else "tap_row"
        row = getattr(self, rowkey)
        P_, F_ = ap.shape[0], int(np.prod(ap.shape[1:]))
        self.tap_map[name] = (row, P_, F_, isbf)
        setattr(self, rowkey, row + P_)
        return self.P.op("gpsimd", (lambda a=ap, r=row, p=P_, f=F_, d_=dst: lambda e: e.dma_start(
            out=d_.ap()[r:r + p, 0:f], in_=a))(), wait=wait, dma=True)

    def par_dma(self, fn0, fn1):
        def fn(e, sem):
            rg = self.regs
            with e.If_eq(rg["rpar"], 0):
                fn0(e).then_inc(sem, 16)
            with e.Else():
                fn1(e).then_inc(sem, 16)
            return None
        return fn

    def flag_write(self, idx, wait):
        fn = self.par_dma(
            lambda e: e.dma_start(out=self.FLG.ap()[0:1, idx:idx + 1], in_=self.PARN[0:1, 1:2]),
            lambda e: e.dma_start(out=self.FLG.ap()[1:2, idx:idx + 1], in_=self.PARN[0:1, 1:2]))
        return self.P.op("gpsimd", fn, wait=list(wait) + [self.toks["regs"]], dma=True, manual=True, stream="flags")

    def flag_poll(self, idx, extra=0):
        P = self.P

        def fn2(e):
            rg = self.regs
            for par_i, src_ap in ((0, self.FLG.ap()[1:2, idx:idx + 1]),
                                  (1, self.FLG.ap()[0:1, idx:idx + 1])):
                guard = e.If_eq(rg["rpar"], par_i) if par_i == 0 else e.Else()
                with guard:
                    e.reg_mov(rg["rcond"], 1)
                    with e.While(rg["rcond"]):
                        e.dma_start(out=self.FLT[:], in_=src_ap).then_inc(self.psem, 16)
                        e.reg_add(rg["rcnt"], rg["rcnt"], 16)
                        e.wait_ge(self.psem, rg["rcnt"])
                        e.reg_load(rg["rflag"], self.FLT[0:1, 0:1])
                        e.reg_alu(rg["rcond"], rg["rflag"], rg["rnon"], ALU.bitwise_xor)
            return e.nop()
        return P.op("gpsimd", fn2, wait=[self.toks["regs"]])

    def ph_quant(self):
        P, AR = self.P, self.AR
        AR.reset()
        W32 = [AR.take(128, 2048, F32) for _ in range(2)]
        WBF = [AR.take(128, 2048, BF16) for _ in range(2)]
        GS = [AR.take(128, 32, F32) for _ in range(2)]
        GS2 = [AR.take(128, 32, F32) for _ in range(2)]
        SCB = [AR.take(128, 32, BF16) for _ in range(2)]
        THR = [AR.take(128, 32, F32) for _ in range(2)]
        M = [AR.take(128, 2048, F32) for _ in range(2)]
        ABSQ = [AR.take(128, 2048, F32) for _ in range(2)]
        SGN = [AR.take(128, 2048, BF16) for _ in range(2)]
        TT = [AR.take(128, 2048, F32) for _ in range(2)]
        WH = [AR.take(128, 2048, BF16) for _ in range(2)]
        b3 = lambda ap: ap.rearrange("p (g o) -> p g o", o=1).to_broadcast([128, 32, 64])
        v3 = lambda ap: ap.rearrange("p (g k) -> p g k", g=32)

        units = []
        for r in range(HF // 128):
            units.append(("gu", self.wgu_sh.ap()[r * 128:(r + 1) * 128, :],
                          16777216, r * 262144, self.WQGU, 2048))
        for r in range(1024 // 128):
            for cc in range(4):
                units.append(("p", self.wp_sh.ap()[r * 128:(r + 1) * 128, cc * 2048:(cc + 1) * 2048],
                              8388608, r * 128 * 8192 + cc * 2048, self.WQP, 8192))
        last = {}
        t_whd_last = None
        for i, (kind, src, parmul, off, dst, dstride) in enumerate(units):
            u = i % 2
            t_load = P.op("sync", (lambda s_=src, w=W32[u]: lambda e: e.dma_start(out=w, in_=s_))(),
                          wait=[last.get(("cast", u))], dma=True)
            t_cast = P.op("gpsimd", (lambda w=W32[u], o=WBF[u]: lambda e: e.tensor_copy(o, w))(),
                          wait=[t_load, last.get(("gs", u)), last.get(("m", u)), last.get(("sgn", u))])
            t_gs = P.op("vector", (lambda w=WBF[u], o=GS[u]: lambda e: e.tensor_reduce(
                o.rearrange("p (g o) -> p g o", o=1), v3(w), AX, ALU.add, apply_absolute_value=True))(),
                wait=[t_cast, last.get(("sc1", u))])
            t_sc1 = P.op("vector", (lambda a=GS[u], o=GS2[u]: lambda e: e.tensor_scalar(
                o, a, 1.0 / 64, 1e-8, ALU.mult, ALU.max))(), wait=[t_gs, last.get(("scb", u))])
            t_scb = P.op("gpsimd", (lambda a=GS2[u], o=SCB[u]: lambda e: e.tensor_copy(o, a))(),
                         wait=[t_sc1, last.get(("thr", u)), last.get(("tt", u))])
            t_thr = P.op("vector", (lambda a=SCB[u], o=THR[u]: lambda e: e.tensor_scalar_mul(
                o, a, float(0.5 + 2.0 ** -10)))(), wait=[t_scb, last.get(("m", u))])
            t_abs = P.op("scalar", (lambda w=WBF[u], o=ABSQ[u]: lambda e: e.activation(
                o, w, AF.Abs))(), wait=[t_cast, last.get(("absr", u))])
            t_m = P.op("vector", (lambda a=ABSQ[u], th=THR[u], o=M[u]: lambda e: e.tensor_tensor(
                v3(o), v3(a), b3(th), ALU.is_gt))(),
                wait=[t_thr, t_abs, last.get(("tt", u))])
            last[("absr", u)] = t_m
            t_sgn = P.op("scalar", (lambda w=WBF[u], o=SGN[u]: lambda e: e.sign(o, w))(),
                         wait=[t_cast, last.get(("wh", u))])
            t_tt = P.op("gpsimd", (lambda m=M[u], sc=SCB[u], o=TT[u]: lambda e: e.tensor_tensor(
                v3(o), v3(m), b3(sc), ALU.mult))(), wait=[t_m, last.get(("wh", u))])
            t_wh = P.op("vector", (lambda t=TT[u], sg=SGN[u], o=WH[u]: lambda e: e.tensor_tensor(
                o, t, sg, ALU.mult))(), wait=[t_tt, t_sgn, last.get(("whd", u))])

            whd_fn = self.par_dma(
                (lambda of=off, d=dst, ds=dstride, w=WH[u]: lambda e: e.dma_start(
                    out=bass.AP(d, of, [[ds, 128], [1, 2048]]), in_=w))(),
                (lambda pm=parmul, of=off, d=dst, ds=dstride, w=WH[u]: lambda e: e.dma_start(
                    out=bass.AP(d, pm + of, [[ds, 128], [1, 2048]]), in_=w))())
            t_whd = P.op("gpsimd", whd_fn, wait=[t_wh, self.toks["regs"]], dma=True, manual=True, stream="whd")
            last.update({("cast", u): t_cast, ("gs", u): t_gs, ("sc1", u): t_sc1,
                         ("scb", u): t_scb, ("thr", u): t_thr, ("m", u): t_m,
                         ("sgn", u): t_sgn, ("tt", u): t_tt, ("wh", u): t_wh,
                         ("whd", u): t_whd})
            t_whd_last = t_whd
        self.t_quant_flag = self.flag_write(0, wait=[t_whd_last])

    def ph_inproj(self):
        P, AR = self.P, self.AR
        AR.reset()
        UFM = AR.take(128, 32768, BF16)
        WZQ = AR.take(128, 8192, BF16)
        WXB = [AR.take(128, 2048, BF16) for _ in range(2)]
        USTG = AR.take(128, 2048, F32)
        UCST = AR.take(128, 2048, BF16)
        EZ = [AR.take(128, 512, BF16) for _ in range(2)]
        DTE = AR.take(32, 512, F32)
        WOF = AR.take(128, 2048, F32)
        WOB = AR.take(128, 2048, BF16)
        HZ = AR.take(128, 4, BF16)
        self._wos = (WOF, WOB)
        ufm = lambda k: UFM[:, k * 2048:(k + 1) * 2048]
        TBF = [self.psum_bf(6), self.psum_bf(7)]

        t_hz = P.op("vector", lambda e: e.memset(HZ[:], 0.0))
        for j in range(NCT):
            P.op("gpsimd", (lambda jj=j: lambda e: e.dma_start(
                out=self.XBCS.ap()[jj * 128:(jj + 1) * 128, 0:3], in_=HZ[:, 0:3]))(),
                wait=[t_hz], dma=True)

        last_tp = {0: None, 1: None}
        last_ez = {0: None, 1: None}
        last_wzmm = None
        last_wxmm = {0: None, 1: None}
        last_ps01 = {0: None, 1: None}
        last_ps23 = {0: None, 1: None}
        last_dte = None
        t_cast_prev = None
        t_lastev_prev = None
        for hf in range(2):
            tok_ev = {}
            for tt in range(16):
                tg = hf * 16 + tt
                t_u = P.op("sync", (lambda r=tg: lambda e: e.dma_start(
                    out=USTG, in_=self.u.ap()[r * 128:(r + 1) * 128, :]))(),
                    wait=[t_cast_prev], dma=True)
                t_cast = P.op("vector", lambda e: e.tensor_copy(UCST, USTG),
                              wait=[t_u, t_lastev_prev])
                t_cast_prev = t_cast
                for k in range(16):
                    sl = k % 2
                    t_tp = P.op("tensor", (lambda kk=k, p=TBF[sl]: lambda e: e.transpose(
                        p[0:128, 0:128], UCST[:, kk * 128:(kk + 1) * 128], self.IDB[:]))(),
                        wait=[t_cast, self.toks["idb"], last_tp[sl]])
                    eng = "vector" if k % 2 else "scalar"
                    if eng == "vector":
                        fn = (lambda kk=k, p=TBF[sl], t_=tt: lambda e: e.tensor_copy(
                            ufm(kk)[:, t_ * 128:(t_ + 1) * 128], p[0:128, 0:128]))()
                    else:
                        fn = (lambda kk=k, p=TBF[sl], t_=tt: lambda e: e.copy(
                            ufm(kk)[:, t_ * 128:(t_ + 1) * 128], p[0:128, 0:128]))()
                    tok_ev[(tt, k)] = P.op(eng, fn, wait=[t_tp])
                    last_tp[sl] = tok_ev[(tt, k)]
                t_lastev_prev = tok_ev[(tt, 15)]

            for q in range(4):
                t_wz = P.op("gpsimd", (lambda qq=q: lambda e: e.dma_start(
                    out=WZQ.rearrange("p (a f) -> p a f", a=16),
                    in_=self.wz.ap()[:, qq * 512:(qq + 1) * 512].rearrange(
                        "(a p) f -> p a f", p=128)))(),
                    wait=[last_wzmm], dma=True)
                for tt in range(16):
                    pp = self.PS[tt % 2]
                    ppev = {}
                    t_mm = None
                    for k in range(16):
                        t_mm = P.op("tensor", (lambda kk=k, t_=tt, p=pp: lambda e: e.matmul(
                            p[:, 0:512], ufm(kk)[:, t_ * 128:(t_ + 1) * 128],
                            WZQ[:, kk * 512:(kk + 1) * 512],
                            start=(kk == 0), stop=(kk == 15)))(),
                            wait=([t_wz, tok_ev[(tt, k)]]
                                  + ([last_ps01[tt % 2]] if k == 0 else [])),
                            silent=(k < 15))
                    last_wzmm = t_mm
                    t_ev = P.op("vector", (lambda p=pp, u=EZ[tt % 2]: lambda e: e.tensor_copy(u, p[:, 0:512]))(),
                                wait=[t_mm, last_ez[tt % 2]])
                    last_ps01[tt % 2] = t_ev
                    t_dma = P.op("sync", (lambda u=EZ[tt % 2], r=hf * 2048 + tt * 128, qq=q:
                                 lambda e: e.dma_start(out=self.ZS.ap()[r:r + 128, qq * 512:(qq + 1) * 512], in_=u))(),
                                 wait=[t_ev], dma=True)
                    last_ez[tt % 2] = t_dma

            for j in range(NCT + 1):
                w = 128 if j < NCT else NH
                t_wx = P.op("gpsimd", (lambda jj=j, ww=w, b=WXB[j % 2]: lambda e: e.dma_start(
                    out=b[:, 0:16 * ww].rearrange("p (a f) -> p a f", a=16),
                    in_=self.wxbc.ap()[:, jj * 128:jj * 128 + ww].rearrange(
                        "(a p) f -> p a f", p=128)))(),
                    wait=[last_wxmm[j % 2]], dma=True)
                for tb in range(4):
                    pp = self.PS[2 + tb % 2]
                    t_mm = None
                    for k in range(16):
                        t_mm = P.op("tensor", (lambda kk=k, b=WXB[j % 2], ww=w, t_=tb, p=pp: lambda e: e.matmul(
                            p[0:ww, 0:512], b[:, kk * ww:(kk + 1) * ww],
                            ufm(kk)[:, t_ * 512:(t_ + 1) * 512],
                            start=(kk == 0), stop=(kk == 15)))(),
                            wait=([t_wx, tok_ev[(4 * tb + 3, k)]]
                                  + ([last_ps23[tb % 2]] if k == 0 else [])),
                            silent=(k < 15))
                    last_wxmm[j % 2] = t_mm
                    if j < NCT:
                        t_ev = P.op("scalar", (lambda p=pp, u=EZ[tb % 2]: lambda e: e.copy(u, p[0:128, 0:512]))(),
                                    wait=[t_mm, last_ez[tb % 2]])
                        t_dma = P.op("sync", (lambda u=EZ[tb % 2], r=j * 128, c=3 + hf * 2048 + tb * 512:
                                     lambda e: e.dma_start(out=self.XBCS.ap()[r:r + 128, c:c + 512], in_=u))(),
                                     wait=[t_ev], dma=True)
                        last_ez[tb % 2] = t_dma
                        last_ps23[tb % 2] = t_ev
                    else:
                        t_ev = P.op("vector", (lambda p=pp: lambda e: e.tensor_copy(DTE, p[0:NH, 0:512]))(),
                                    wait=[t_mm, last_dte])
                        t_dma = P.op("sync", (lambda c=hf * 2048 + tb * 512:
                                     lambda e: e.dma_start(out=self.DTS.ap()[:, c:c + 512], in_=DTE))(),
                                     wait=[t_ev], dma=True)
                        last_dte = t_dma
                        last_ps23[tb % 2] = t_ev

    def ph_wosprep(self):
        P = self.P
        WOF, WOB = self._wos
        tprev = None
        for kk in range(16):
            t_l = P.op("sync", (lambda r=kk: lambda e: e.dma_start(
                out=WOF, in_=self.wout.ap()[r * 128:(r + 1) * 128, :]))(),
                wait=[tprev], dma=True)
            t_s = P.op("vector", (lambda r=kk: lambda e: e.tensor_scalar(
                WOB, WOF, self.GCOL[:, r:r + 1], None, ALU.mult))(),
                wait=[t_l, self.toks["gcol"], tprev])
            t_d = P.op("sync", (lambda r=kk: lambda e: e.dma_start(
                out=self.WOS.ap()[r * 128:(r + 1) * 128, :], in_=WOB))(),
                wait=[t_s], dma=True)
            tprev = t_d

    def ph_ssd(self):
        P, AR = self.P, self.AR
        AR.reset()
        XC = AR.take(128, 6 * 520, BF16)
        CFM = AR.take(128, 16 * 512, BF16)
        XFM = AR.take(128, 4 * 512, BF16)
        BFM = AR.take(128, 4 * 512, BF16)
        BFMD = AR.take(128, 8 * 512, BF16)
        DXT = AR.take(128, 2048, BF16)
        TY = [AR.take(128, 64, F32) for _ in range(2)]
        XTM = AR.take(128, 4 * 512, BF16)
        BTM = AR.take(128, 4 * 512, BF16)
        ZTM = AR.take(128, 2048, BF16)
        SZ = AR.take(128, 2048, F32)
        XDT = AR.take(128, 2048, BF16)
        BDT = AR.take(128, 2048, BF16)
        EXPD = [AR.take(128, 128, F32) for _ in range(2)]
        LT = [AR.take(128, 128, BF16) for _ in range(2)]
        CEX = [AR.take(128, 128, BF16) for _ in range(2)]
        YZ = AR.take(128, 2048, F32)
        YB = AR.take(128, 2048, F32)
        RMSA = AR.take(128, 32, F32)
        RMSB = AR.take(128, 32, F32)
        CONVA = AR.take(128, 512, F32)
        CONVB = AR.take(128, 512, F32)
        CONVT = AR.take(128, 512, F32)
        NT = AR.take(128, 2048, BF16)
        NFM = AR.take(128, 16 * 512, BF16)
        WOBT = AR.take(128, 8 * 512, BF16)
        EOB = [AR.take(128, 512, F32) for _ in range(2)]
        H1, H2 = self.HA, self.HB
        TBF = [self.psum_bf(6), self.psum_bf(7)]

        rt = {}
        g2tile = lambda h: (h // 4)
        tk = self.toks
        t_h0 = P.op("vector", lambda e: e.memset(H1[:], 0.0))
        rt["hstt_last"] = t_h0

        for g in range(NSC):
            T0 = g * 512
            t_xc = {}
            for half in range(4):
                t_xc[half] = P.op("sync", (lambda hh=half, t0=T0: lambda e: e.dma_start(
                    out=XC.rearrange("p (j t) -> p j t", j=6)[:, 0:6, 0:515],
                    in_=self.XBCS.ap()[hh * 6 * 128:(hh + 1) * 6 * 128, t0:t0 + 515]
                        .rearrange("(j p) t -> p j t", p=128)))(),
                    wait=[rt.get("xcread")], dma=True)
                for jl in range(6):
                    j = half * 6 + jl
                    tap = lambda k, jj=jl: XC[:, jj * 520 + k:jj * 520 + k + 512]
                    wc = lambda k, jj=j: self.CONVW[:, jj * 4 + k:jj * 4 + k + 1]
                    if j < 4:
                        dest = XFM[:, j * 512:(j + 1) * 512]
                        dkey = "xfm"
                    elif j < 8:
                        dest = BFM[:, (j - 4) * 512:(j - 3) * 512]
                        dkey = "bfm"
                    else:
                        dest = CFM[:, (j - 8) * 512:(j - 7) * 512]
                        dkey = ("cfm", j - 8)
                    t1 = P.op("vector", (lambda t=tap(0), w=wc(0), jj=j: lambda e: e.tensor_scalar(
                        CONVA, t, w, self.CONVB[:, jj:jj + 1], ALU.mult, ALU.add))(),
                        wait=[t_xc[half], tk["cw"], tk["cb"], rt.get("conva")])
                    t2m = P.op("gpsimd", (lambda t=tap(1), w=wc(1): lambda e: e.tensor_tensor(
                        CONVT, t, w.to_broadcast([128, 512]), ALU.mult))(),
                        wait=[t_xc[half], tk["cw"], rt.get("convt")])
                    t2 = P.op("gpsimd", lambda e: e.tensor_add(CONVB, CONVT, CONVA),
                              wait=[t1, t2m, rt.get("convb")])
                    rt["convt"] = t2
                    t3 = P.op("vector", (lambda t=tap(2), w=wc(2): lambda e: e.scalar_tensor_tensor(
                        CONVA, t, w, CONVB, ALU.mult, ALU.add))(), wait=[t2])
                    t4m = P.op("gpsimd", (lambda t=tap(3), w=wc(3): lambda e: e.tensor_tensor(
                        CONVT, t, w.to_broadcast([128, 512]), ALU.mult))(),
                        wait=[t2, rt.get("convt")])
                    t4 = P.op("gpsimd", lambda e: e.tensor_add(CONVB, CONVT, CONVA),
                              wait=[t3, t4m])
                    rt["convt"] = t4
                    rt["conva"] = t4
                    t5 = P.op("scalar", (lambda d=dest: lambda e: e.activation(
                        d, CONVB, AF.Silu))(),
                        wait=[t4, rt.get(("cread", dkey)), rt.get(("cread2", dkey))])
                    rt["convb"] = t5
                    rt[("csilu", dkey)] = t5
                    rt["xcread"] = t4
            if g == 0:
                self.tap("xfm", XFM, [rt.get(("csilu", "xfm"))])
                self.tap("bfm", BFM, [rt.get(("csilu", "bfm"))])
                self.tap("cfm0", CFM[:, 0:512], [rt.get(("csilu", ("cfm", 0)))])
            bdup_tok = []
            for t in range(8):
                for b2 in range(2):
                    tok = P.op("gpsimd" if (t + b2) % 2 else "scalar",
                               (lambda t_=t, b_=b2: lambda e: e.dma_start(
                                   out=BFMD[b_ * 64:b_ * 64 + 64, t_ * 512:(t_ + 1) * 512],
                                   in_=BFM[(t_ % 2) * 64:(t_ % 2) * 64 + 64,
                                           (t_ // 2) * 512:(t_ // 2) * 512 + 512]))(),
                               wait=[rt.get(("csilu", "bfm")), rt.get("bfmdread")], dma=True)
                    bdup_tok.append(tok)
            rt["bdup"] = bdup_tok

            xtm_tok = {}
            for c4 in range(4):
                for src, dst, name in ((XFM, XTM, "xfm"), (BFM, BTM, "bfm")):
                    evs = []
                    for jj in range(4):
                        sl = (c4 * 8 + (0 if name == "xfm" else 4) + jj) % 2
                        t_tp = P.op("tensor", (lambda s=src, j_=jj, c_=c4, p=TBF[sl]: lambda e: e.transpose(
                            p[0:128, 0:128], s[:, j_ * 512 + c_ * 128:j_ * 512 + c_ * 128 + 128],
                            self.IDB[:]))(),
                            wait=[rt.get(("csilu", name)), tk["idb"], rt.get(("psy", sl)),
                                  rt.get(("tbf", sl))])
                        eng = "vector" if jj % 2 else "scalar"
                        if eng == "vector":
                            fn = (lambda d=dst, j_=jj, c_=c4, p=TBF[sl]: lambda e: e.tensor_copy(
                                d[:, c_ * 512 + j_ * 128:c_ * 512 + j_ * 128 + 128], p[0:128, 0:128]))()
                        else:
                            fn = (lambda d=dst, j_=jj, c_=c4, p=TBF[sl]: lambda e: e.copy(
                                d[:, c_ * 512 + j_ * 128:c_ * 512 + j_ * 128 + 128], p[0:128, 0:128]))()
                        t_ev = P.op(eng, fn, wait=[t_tp, rt.get(("tmread", name, c4))])
                        rt[("tbf", sl)] = t_ev
                        rt[("cread", name)] = t_tp
                        evs.append(t_ev)
                    xtm_tok[(c4, name)] = evs
            for c4 in range(4):
                ci = g * 4 + c4
                cp = ci % 2
                Hrd = H1 if cp == 0 else H2
                Hwr = H2 if cp == 0 else H1
                t_dt = P.op("sync", (lambda c=ci: lambda e: e.dma_start(
                    out=self.DTCH[:], in_=self.DTS.ap()[:, c * 128:(c + 1) * 128]))(),
                    wait=[rt.get("dtread")], dma=True)
                SPT0 = self.SPT[0:NH, 0:128]
                SPT1 = self.SPT[0:NH, 128:256]
                SPT2 = self.SPT[0:NH, 256:384]
                t_s1 = P.op("vector", lambda e: e.tensor_scalar(
                    SPT0, self.DTCH[:], self.DTBC[:, 0:1], None, ALU.add),
                    wait=[t_dt, tk["dtbc"], rt.get("spt0")])
                rt["dtread"] = t_s1
                t_s2 = P.op("scalar", lambda e: e.activation(
                    SPT1, SPT0, AF.Abs), wait=[t_s1, rt.get("spt1")])
                t_s3 = P.op("scalar", lambda e: e.activation(SPT1, SPT1, AF.Exp, scale=-1.0),
                            wait=[t_s2])
                t_s4 = P.op("scalar", lambda e: e.activation(SPT1, SPT1, AF.Ln, bias=1.0),
                            wait=[t_s3])
                t_s5 = P.op("vector", lambda e: e.tensor_scalar(
                    SPT2, SPT0, 0.0, None, ALU.max),
                    wait=[t_s1, rt.get("spt2")])
                rt["spt0"] = t_s5
                t_dtf = P.op("vector", lambda e: e.tensor_add(
                    self.DTF[:], SPT2, SPT1), wait=[t_s5, t_s4, rt.get("dtf_war")])
                rt["spt1"] = t_dtf
                rt["spt2"] = t_dtf
                t_da = P.op("vector", lambda e: e.tensor_scalar(
                    SPT0, self.DTF[:], self.ACOL[:, 0:1], None, ALU.mult),
                    wait=[t_dtf, tk["acol"]])
                t_cum = P.op("vector", (lambda: lambda e: e.tensor_tensor_scan(
                    self.CUM[:], SPT0, self.COLS[0:NH, 2:3].to_broadcast([NH, 128]),
                    0.0, ALU.add, ALU.add))(),
                    wait=[t_da, rt.get("cum_war"), self.t_zero_col])
                t_dec = P.op("scalar", lambda e: e.activation(
                    self.DEC[:], self.CUM[:], AF.Exp, scale=-1.0,
                    bias=self.CUM[:, 127:128]), wait=[t_cum, rt.get("dec_war")])
                t_exc = P.op("scalar", lambda e: e.activation(
                    self.EXPC[:], self.CUM[:], AF.Exp), wait=[t_cum, rt.get("expc")])
                t_b1 = P.op("gpsimd", lambda e: e.dma_start(
                    out=self.BRS.ap()[0:32, :], in_=self.CUM[:]),
                    wait=[t_cum, rt.get("brsread")], dma=True)
                t_b2 = P.op("gpsimd", lambda e: e.dma_start(
                    out=self.BRS.ap()[32:64, :], in_=self.EXPC[:]),
                    wait=[t_exc, rt.get("brsread")], dma=True)
                rt["expc"] = t_b2
                t_br = P.op("gpsimd", lambda e: e.dma_start(
                    out=self.BR[:], in_=self.BRS.ap().rearrange("a b -> (a b)")
                    .rearrange("(o f) -> o f", o=1)),
                    wait=[t_b1, t_b2, rt.get("br_sb_read")], dma=True)
                rt["brsread"] = t_br
                evs_t = []
                for src_t, col, wt in ((self.CUM, 0, [t_cum]), (self.DTF, 32, [t_dtf]),
                                       (self.DEC, 64, [t_dec])):
                    t_stp = P.op("tensor", (lambda s_=src_t: lambda e: e.transpose(
                        self.PS[2][:, 0:32], s_[:], self.IDF[0:32, 0:32]))(),
                        wait=wt + [tk["idf"], rt.get(("psc", 0))])
                    t_tev = P.op("vector", (lambda c_=col: lambda e: e.tensor_copy(
                        self.TCOL[:, c_:c_ + 32], self.PS[2][:, 0:32]))(),
                        wait=[t_stp, rt.get("tcolread")])
                    rt[("psc", 0)] = t_tev
                    evs_t.append(t_tev)
                t_tcol = evs_t[-1]
                rt["cum_war"] = [evs_t[0], t_dec, t_exc, t_b1]
                rt["dtf_war"] = evs_t[1]
                rt["dec_war"] = evs_t[2]
                t_negt = P.op("vector", lambda e: e.tensor_scalar_mul(
                    self.NEGT[:], self.TCOL[:, 0:32], -1.0), wait=[evs_t[0], rt.get("negtread")])
                t_cdmm = P.op("tensor", (lambda: lambda e: e.matmul(
                    self.PS[3][0:64, 0:NH], self.ONE1[:, 0:64],
                    self.BR[0:1, 4096:8192].rearrange("o (h t) -> o h t", h=NH)[:, :, 127:128]
                    .rearrange("o h t -> o (h t)"),
                    start=True, stop=True))(),
                    wait=[t_br, tk["one"], rt.get(("psc", 1))])
                t_cdec = P.op("vector", lambda e: e.tensor_copy(self.CDECB[:], self.PS[3][0:64, 0:NH]),
                              wait=[t_cdmm, rt.get("cdecread")])
                rt[("psc", 1)] = t_cdec
                t_ztm = P.op("sync", (lambda c=ci: lambda e: e.dma_start(
                    out=ZTM, in_=self.ZS.ap()[c * 128:(c + 1) * 128, :]))(),
                    wait=[rt.get("szread_z")], dma=True)
                t_sz = P.op("scalar", lambda e: e.activation(SZ, ZTM, AF.Silu),
                            wait=[t_ztm, rt.get("szread")])
                rt["szread_z"] = t_sz
                t_hbf0 = P.op("gpsimd", (lambda hr=Hrd: lambda e: e.tensor_copy(
                    self.HBF[0:64, :], hr[:]))(),
                    wait=[rt.get("hstt_last"), rt.get("hbfread"), rt.get("hbfdup")])
                t_hbf = P.op("gpsimd", lambda e: e.dma_start(
                    out=self.HBF[64:128, :], in_=self.HBF[0:64, :]),
                    wait=[t_hbf0, rt.get("hbfread")], dma=True)
                rt["hbfdup"] = t_hbf
                xtm4 = XTM[:, c4 * 512:(c4 + 1) * 512].rearrange(
                    "t (g o p) -> t g o p", g=NG, o=1).to_broadcast([128, NG, 4, 64])
                btm4 = BTM[:, c4 * 512:(c4 + 1) * 512].rearrange(
                    "t (g o p) -> t g o p", g=NG, o=1).to_broadcast([128, NG, 4, 64])
                dt4 = self.TCOL[:, 32:64].rearrange("t (g r o) -> t g r o", g=NG, r=4, o=1)\
                    .to_broadcast([128, NG, 4, 64])
                de4 = self.TCOL[:, 64:96].rearrange("t (g r o) -> t g r o", g=NG, r=4, o=1)\
                    .to_broadcast([128, NG, 4, 64])
                xdt4 = XDT.rearrange("t (g r p) -> t g r p", g=NG, r=4)
                bdt4 = BDT.rearrange("t (g r p) -> t g r p", g=NG, r=4)
                t_xdt = P.op("vector", (lambda a=xtm4, b=dt4, o=xdt4: lambda e: e.tensor_tensor(
                    o, a, b, ALU.mult))(),
                    wait=[t_tcol, rt.get("xdtread")] + xtm_tok[(c4, "xfm")])
                dbc4 = self.DBC.rearrange("t (g r o) -> t g r o", g=NG, r=4, o=1)\
                    .to_broadcast([128, NG, 4, 64])
                t_dxt = P.op("gpsimd", (lambda a=xtm4, b=dbc4: lambda e: e.tensor_tensor(
                    DXT.rearrange("t (g r p) -> t g r p", g=NG, r=4), a, b, ALU.mult))(),
                    wait=[tk["dbc"], rt.get("dxtread")] + xtm_tok[(c4, "xfm")])
                t_bdt = P.op("vector", (lambda a=btm4, b=de4, o=bdt4: lambda e: e.tensor_tensor(
                    o, a, b, ALU.mult))(),
                    wait=[t_tcol, rt.get("bdtread")] + xtm_tok[(c4, "bfm")])
                rt["tcolread"] = t_bdt
                rt[("tmread", "xfm", c4)] = t_xdt
                rt[("tmread", "bfm", c4)] = t_bdt

                if ci == 0 and self.taps_on:
                    pass
                if ci == 0:
                    self.tap("cum", self.CUM[:], [t_cum])
                    self.tap("dtf", self.DTF[:], [t_dtf])
                    self.tap("dec", self.DEC[:], [t_dec])
                    self.tap("tcol", self.TCOL[:], [t_tcol])
                    self.tap("negt", self.NEGT[:], [t_negt])
                    self.tap("cdecb", self.CDECB[:], [t_cdec])
                    self.tap("br", self.BR[0:1, :], [t_br])
                    self.tap("xdt", XDT, [t_xdt])
                    self.tap("bdt", BDT, [t_bdt])
                    self.tap("dxt", DXT, [t_dxt])
                    self.tap("xtm0", XTM[:, 0:512], xtm_tok[(0, "xfm")])
                    self.tap("sz", SZ[:, 0:2048], [t_sz])
                t_dblk = {}
                for h in range(NH):
                    blk, sl, bsl = h // 4, h % 2, (h // 4) % 2
                    g2 = g2tile(h)
                    b0 = (h % 2) * 64
                    if h % 4 == 0:
                        P.op("tensor", (lambda b=blk, p=self.PS[bsl]: lambda e: e.matmul(
                            p[:, 0:512], self.ONE1[:], self.BR[0:1, b * 512:(b + 1) * 512],
                            start=True, stop=False))(),
                            wait=[t_br, tk["one"], rt.get(("psd", bsl))], silent=True)
                        t_dblk[blk] = P.op("tensor", (lambda p=self.PS[bsl]: lambda e: e.matmul(
                            p[:, 0:512], self.IDF[:], self.MSK[:],
                            start=False, stop=True))(), wait=[tk["idf"], tk["msk"]])
                    if self.taps_on and ci == 0 and h == 0:
                        t_dcp = P.op("vector", lambda e: e.tensor_copy(CONVT, self.PS[0][:, 0:512]),
                                     wait=[t_dblk[0], rt.get("convt")])
                        t_dtap = self.tap("dpsum", CONVT, [t_dcp])
                        rt["convt"] = t_dtap
                    t_expd = P.op("scalar", (lambda hh=h, b=blk, s_=sl, p=self.PS[bsl]: lambda e: e.activation(
                        EXPD[s_], p[:, (hh % 4) * 128:(hh % 4 + 1) * 128], AF.Exp,
                        bias=self.NEGT[:, hh:hh + 1]))(),
                        wait=[t_dblk[blk], t_negt, rt.get(("expdread", sl))])
                    rt[("psd", bsl)] = t_expd
                    t_cxm = P.op("tensor", (lambda hh=h, s_=sl: lambda e: e.matmul(
                        self.PS[2 + s_][:, 0:128], self.ONE1[:],
                        self.BR[0:1, 4096 + hh * 128:4096 + (hh + 1) * 128],
                        start=True, stop=True))(),
                        wait=[t_br, rt.get(("psc", sl))])
                    if self.taps_on and ci == 0 and h == 0:
                        t_ccp = P.op("vector", lambda e: e.tensor_copy(CONVT[:, 0:128], self.PS[2][:, 0:128]),
                                     wait=[t_cxm, rt.get("convt")])
                        rt["convt"] = self.tap("cxpsum", CONVT[:, 0:128], [t_ccp])
                    t_cxe = P.op("vector", (lambda hh=h, s_=sl, b_=b0, c_=c4: lambda e: e.scalar_tensor_tensor(
                        CEX[s_][b_:b_ + 64, :], self.PS[2 + s_][b_:b_ + 64, 0:128], 1.0,
                        CFM[b_:b_ + 64,
                            (hh // 2) * 512 + c_ * 128:(hh // 2) * 512 + c_ * 128 + 128],
                        ALU.mult, ALU.mult))(),
                        wait=[t_cxm, rt.get(("csilu", ("cfm", h // 2))), rt.get(("cexread", sl))])
                    rt[("psc", sl)] = t_cxe
                    t_gmm = P.op("tensor", (lambda hh=h, s_=sl, gg=g2, c_=c4, b_=b0: lambda e: e.matmul(
                        self.PS[4 + s_][:, 0:128],
                        BFMD[b_:b_ + 64, gg * 512 + c_ * 128:gg * 512 + c_ * 128 + 128],
                        CFM[b_:b_ + 64,
                            (hh // 2) * 512 + c_ * 128:(hh // 2) * 512 + c_ * 128 + 128],
                        start=True, stop=True))(),
                        wait=bdup_tok + [rt.get(("csilu", ("cfm", h // 2))), rt.get(("psg", sl))])
                    t_lev = P.op("vector", (lambda s_=sl: lambda e: e.tensor_tensor(
                        LT[s_], self.PS[4 + s_][:, 0:128], EXPD[s_], ALU.mult))(),
                        wait=[t_gmm, t_expd, rt.get(("ltread", sl))])
                    rt[("psg", sl)] = t_lev
                    rt[("expdread", sl)] = t_lev
                    t_y1 = P.op("tensor", (lambda hh=h, s_=sl: lambda e: e.matmul(
                        self.PS[6 + s_][:, 0:64], LT[s_], XDT[:, hh * 64:(hh + 1) * 64],
                        start=True, stop=True))(),
                        wait=[t_lev, t_xdt, rt.get(("psy", sl))])
                    t_y2 = P.op("tensor", (lambda hh=h, s_=sl, b_=b0: lambda e: e.matmul(
                        self.PS[6 + s_][:, 64:128], CEX[s_][b_:b_ + 64, :],
                        self.HBF[b_:b_ + 64, hh * 64:(hh + 1) * 64],
                        start=True, stop=True))(), wait=[t_cxe, t_hbf])
                    rt[("ltread", sl)] = t_y2
                    rt[("cexread", sl)] = t_y2
                    rt[("cread2", ("cfm", h // 2))] = t_y2
                    rt[("cread2", "bfm")] = t_y2
                    t_smm = P.op("tensor", (lambda hh=h, s_=sl: lambda e: e.matmul(
                        self.PS[6 + s_][0:64, 128:192], BDT[:, hh * 64:(hh + 1) * 64],
                        XDT[:, hh * 64:(hh + 1) * 64], start=True, stop=True))(),
                        wait=[t_bdt])
                    t_hst = P.op("vector", (lambda hh=h, s_=sl, hr=Hrd, hw=Hwr: lambda e: e.scalar_tensor_tensor(
                        hw[:, hh * 64:(hh + 1) * 64], hr[:, hh * 64:(hh + 1) * 64],
                        self.CDECB[:, hh:hh + 1], self.PS[6 + s_][0:64, 128:192],
                        ALU.mult, ALU.add))(),
                        wait=[t_smm, t_cdec, rt.get(("hwar", h)), t_hbf0])
                    rt[("hwar", h)] = t_hst
                    t_ty = P.op("vector", (lambda hh=h, s_=sl: lambda e: e.scalar_tensor_tensor(
                        TY[s_], self.PS[6 + s_][:, 64:128], 1.0,
                        DXT[:, hh * 64:(hh + 1) * 64], ALU.mult, ALU.add))(),
                        wait=[t_y2, t_dxt, rt.get(("tyread", sl))])
                    t_yz = P.op("vector", (lambda hh=h, s_=sl: lambda e: e.tensor_tensor(
                        YZ[:, hh * 64:(hh + 1) * 64], TY[s_], self.PS[6 + s_][:, 0:64],
                        ALU.add))(),
                        wait=[t_y1, t_ty, t_smm, rt.get("yzread")])
                    rt[("tyread", sl)] = t_yz
                    rt[("psy", sl)] = t_yz
                    rt["hstt_last"] = t_hst
                    rt["hbfread"] = t_yz
                    if self.taps_on and ci == 0 and h == 0:
                        tt1 = self.tap("expd0", EXPD[0], [t_expd])
                        rt[("expdread", 0)] = tt1
                        tt2 = self.tap("lt0", LT[0], [t_lev])
                        rt[("ltread", 0)] = tt2
                        tt3 = self.tap("cex0", CEX[0], [t_cxe])
                        rt[("cexread", 0)] = tt3
                        tt4 = self.tap("yz0", YZ[:, 0:64], [t_yz])
                        rt["yzread"] = tt4
                    if self.taps_on and ci == 1 and h == 0:
                        tt5 = self.tap("hbf1", self.HBF[:, 0:2048], [t_hbf])
                        rt["hbfread"] = tt5
                rt["cdecread"] = t_hst
                rt["negtread"] = t_expd
                rt["br_sb_read"] = t_cxm
                rt["xdtread"] = t_smm
                rt["bdtread"] = t_smm
                rt["dxtread"] = t_ty
                rt["bfmdread"] = t_gmm

                t_yb = P.op("gpsimd", lambda e: e.tensor_tensor(YB, YZ, SZ, ALU.mult),
                            wait=[t_yz, t_sz, rt.get("ybread")])
                rt["yzread"] = t_yb
                rt["szread"] = t_yb
                t_sq = P.op("vector", lambda e: e.tensor_tensor(YZ, YB, YB, ALU.mult),
                            wait=[t_yb])
                t_rs = P.op("vector", (lambda: lambda e: e.tensor_reduce(
                    RMSA.rearrange("p (h o) -> p h o", o=1),
                    YZ.rearrange("p (h k) -> p h k", h=NH), AX, ALU.add))(),
                    wait=[t_sq, rt.get("rmsaread")])
                t_r2 = P.op("vector", lambda e: e.tensor_scalar_mul(RMSB, RMSA, 1.0 / 64),
                            wait=[t_rs, rt.get("rmsbread")])
                t_rl = P.op("scalar", lambda e: e.activation(RMSA, RMSB, AF.Ln, bias=self.COLS[:, 9:10]),
                            wait=[t_r2, self.t_eps_ln])
                rt["rmsaread"] = t_rl
                t_re = P.op("scalar", lambda e: e.activation(RMSB, RMSA, AF.Exp, scale=-0.5),
                            wait=[t_rl])
                rt["rmsbread"] = t_re
                t_nt = P.op("vector", (lambda: lambda e: e.tensor_tensor(
                    NT.rearrange("p (h k) -> p h k", h=NH),
                    YB.rearrange("p (h k) -> p h k", h=NH),
                    RMSB.rearrange("p (h o) -> p h o", o=1).to_broadcast([128, NH, 64]),
                    ALU.mult))(), wait=[t_re, t_yb, rt.get("ntread")])
                rt["ybread"] = t_nt
                ntevs = []
                for kk in range(16):
                    sl2 = kk % 2
                    t_tp = P.op("tensor", (lambda k_=kk, p=TBF[sl2]: lambda e: e.transpose(
                        p[0:128, 0:128], NT[:, k_ * 128:(k_ + 1) * 128], self.IDB[:]))(),
                        wait=[t_nt, rt.get(("psy", sl2)), rt.get(("tbf", sl2))])
                    eng = "vector" if kk % 2 else "scalar"
                    if eng == "vector":
                        fn = (lambda k_=kk, c_=c4, p=TBF[sl2]: lambda e: e.tensor_copy(
                            NFM[:, k_ * 512 + c_ * 128:k_ * 512 + c_ * 128 + 128], p[0:128, 0:128]))()
                    else:
                        fn = (lambda k_=kk, c_=c4, p=TBF[sl2]: lambda e: e.copy(
                            NFM[:, k_ * 512 + c_ * 128:k_ * 512 + c_ * 128 + 128], p[0:128, 0:128]))()
                    t_ev = P.op(eng, fn, wait=[t_tp, rt.get("nfmread")])
                    rt[("tbf", sl2)] = t_ev
                    rt[("psy", sl2)] = t_ev
                    ntevs.append(t_ev)
                rt["ntread"] = ntevs[-1]
                rt[("ntevs", c4)] = ntevs
                if ci == 0:
                    ty1 = self.tap("yz", YZ, [t_yz])
                    if ty1 is not None:
                        rt["yzread"] = ty1
                    ty2 = self.tap("nt", NT, [t_nt])
                    if ty2 is not None:
                        rt["ntread"] = ty2

            if self.taps_on and g == 0:
                tnf = self.tap("nfm01", NFM[:, 0:1024], [rt[("ntevs", 3)][-1]])
                if tnf is not None:
                    rt["nfmread"] = tnf
            nfm_done = [t for c_ in range(4) for t in rt[("ntevs", c_)]]
            t_sha_last = None
            for ob in range(4):
                t_mm_l = {}
                for kkb in range(2):
                    t_wo_l = []
                    for kl in range(8):
                        t_wo_l.append(P.op("sync", (lambda k_=kkb * 8 + kl, kl_=kl, o_=ob: lambda e: e.dma_start(
                            out=WOBT[:, kl_ * 512:(kl_ + 1) * 512],
                            in_=self.WOS.ap()[k_ * 128:(k_ + 1) * 128, o_ * 512:(o_ + 1) * 512]))(),
                            wait=[rt.get("wobtread")], dma=True))
                    for tt in range(4):
                        for kl in range(8):
                            kk = kkb * 8 + kl
                            t_mm_l[tt] = P.op("tensor", (lambda k_=kk, kl_=kl, t_=tt, p=self.PS[tt]: lambda e: e.matmul(
                                p[:, 0:512], NFM[:, k_ * 512 + t_ * 128:k_ * 512 + t_ * 128 + 128],
                                WOBT[:, kl_ * 512:(kl_ + 1) * 512],
                                start=(k_ == 0), stop=(k_ == 15)))(),
                                wait=([t_wo_l[kl]] + nfm_done + ([rt.get(("opps", tt))] if kk == 0 else [])),
                                silent=(kl < 7 or tt < 3))
                    rt["wobtread"] = t_mm_l[3]
                rt["nfmread"] = t_mm_l[3]
                for tt in range(4):
                    t_ev = P.op("vector", (lambda p=self.PS[tt], b=EOB[tt % 2]: lambda e: e.tensor_copy(
                        b, p[:, 0:512]))(), wait=[t_mm_l[3], rt.get(("eob", tt % 2))])
                    rt[("opps", tt)] = t_ev
                    if tt < 2:
                        rt[("psd", tt)] = t_ev
                    else:
                        rt[("psc", tt - 2)] = t_ev
                    base = (T0 + tt * 128) * DM + ob * 512
                    sha_fn = self.par_dma(
                        (lambda of=base, b=EOB[tt % 2]: lambda e: e.dma_start(
                            out=bass.AP(self.SHA, of, [[DM, 128], [1, 512]]), in_=b))(),
                        (lambda of=base + SEQ * DM, b=EOB[tt % 2]: lambda e: e.dma_start(
                            out=bass.AP(self.SHA, of, [[DM, 128], [1, 512]]), in_=b))())
                    t_sha = P.op("gpsimd", sha_fn, wait=[t_ev, tk["regs"]], dma=True, manual=True, stream="sha")
                    rt[("eob", tt % 2)] = t_sha
                    t_sha_last = t_sha
            self.flag_write(1 + g, wait=[t_sha_last])

    def ph_ln1(self):
        P, AR = self.P, self.AR
        AR.reset()
        R1 = AR.take(128, 2048, F32)
        SQ = AR.take(128, 2048, F32)
        LNT = AR.take(128, 2048, BF16)
        M1 = self.COLS[:, 4:5]
        Q1 = self.COLS[:, 5:6]
        T2 = self.COLS[:, 6:7]
        VAR = self.COLS[:, 7:8]
        RSD = self.COLS[:, 8:9]
        rt = {}
        for g_rel in range(4):
            t_poll = self.flag_poll_dyn(1, g_rel)
            for tt in range(4):
                off0 = (0 * 4 + g_rel) * 512 * DM + tt * 128 * DM
                off1 = (1 * 4 + g_rel) * 512 * DM + tt * 128 * DM
                t_u = P.op("gpsimd", self.par_dma(
                    (lambda of=off0: lambda e: e.dma_start(
                        out=R1, in_=bass.AP(self.u, of, [[DM, 128], [1, DM]])))(),
                    (lambda of=off1: lambda e: e.dma_start(
                        out=R1, in_=bass.AP(self.u, of, [[DM, 128], [1, DM]])))()),
                    wait=[rt.get("r1read"), self.toks["regs"]], dma=True, manual=True, stream="r1")
                t_a0 = P.op("gpsimd", self.par_dma(
                    (lambda of=off0: lambda e: e.dma_start(
                        out=R1, in_=bass.AP(self.SHA, of, [[DM, 128], [1, DM]]), accum_op=ALU.add))(),
                    (lambda of=off1: lambda e: e.dma_start(
                        out=R1, in_=bass.AP(self.SHA, of, [[DM, 128], [1, DM]]), accum_op=ALU.add))()),
                    wait=[t_u, t_poll], dma=True, manual=True, stream="r1")
                t_a1 = P.op("gpsimd", self.par_dma(
                    (lambda of=off0 + SEQ * DM: lambda e: e.dma_start(
                        out=R1, in_=bass.AP(self.SHA, of, [[DM, 128], [1, DM]]), accum_op=ALU.add))(),
                    (lambda of=off1 + SEQ * DM: lambda e: e.dma_start(
                        out=R1, in_=bass.AP(self.SHA, of, [[DM, 128], [1, DM]]), accum_op=ALU.add))()),
                    wait=[t_a0], dma=True, manual=True, stream="r1")
                t_m = P.op("vector", lambda e: e.tensor_reduce(
                    M1.rearrange("p (a o) -> p a o", o=1),
                    R1.rearrange("p (a k) -> p a k", a=1), AX, ALU.add),
                    wait=[t_a1, rt.get("m1read")])
                t_sq = P.op("scalar", lambda e: e.activation(SQ, R1, AF.Square),
                            wait=[t_a1, rt.get("sqread")])
                t_q = P.op("vector", lambda e: e.tensor_reduce(
                    Q1.rearrange("p (a o) -> p a o", o=1),
                    SQ.rearrange("p (a k) -> p a k", a=1), AX, ALU.add),
                    wait=[t_sq, rt.get("q1read")])
                rt["sqread"] = t_q
                t_msc = P.op("vector", lambda e: e.tensor_scalar_mul(M1, M1, 1.0 / DM),
                             wait=[t_m, rt.get("m1bread")])
                rt["m1read"] = t_msc
                t_mm2 = P.op("vector", lambda e: e.tensor_tensor(
                    T2, M1, M1, ALU.mult), wait=[t_msc, rt.get("t2read")])
                t_var = P.op("vector", lambda e: e.scalar_tensor_tensor(
                    VAR, Q1, 1.0 / DM, T2, ALU.mult, ALU.subtract),
                    wait=[t_mm2, t_q, rt.get("varread")])
                rt["q1read"] = t_var
                rt["t2read"] = t_var
                t_ln = P.op("scalar", lambda e: e.activation(VAR, VAR, AF.Ln, bias=self.COLS[:, 9:10]),
                            wait=[t_var, self.t_eps_ln])
                t_ex = P.op("scalar", lambda e: e.activation(RSD, VAR, AF.Exp, scale=-0.5),
                            wait=[t_ln, rt.get("rsdread")])
                rt["varread"] = t_ex
                t_lnt = P.op("vector", lambda e: e.tensor_scalar(
                    LNT, R1, M1, RSD, ALU.subtract, ALU.mult),
                    wait=[t_msc, t_ex, rt.get("lntread")])
                rt["rsdread"] = t_lnt
                rt["m1bread"] = t_lnt
                rt["r1read"] = t_lnt
                t_st = P.op("sync", (lambda gr=g_rel, t_=tt: lambda e: e.dma_start(
                    out=self.LN1S.ap()[gr * 512 + t_ * 128:gr * 512 + t_ * 128 + 128, :],
                    in_=LNT))(), wait=[t_lnt], dma=True)
                rt["lntread"] = t_st

    def flag_poll_dyn(self, base, g_rel):
        P = self.P

        def fn(e):
            rg = self.regs
            for par_i in (0, 1):
                idx = base + par_i * 4 + g_rel
                src_ap = self.FLG.ap()[1 - par_i:2 - par_i, idx:idx + 1]
                guard = e.If_eq(rg["rpar"], par_i) if par_i == 0 else e.Else()
                with guard:
                    e.reg_mov(rg["rcond"], 1)
                    with e.While(rg["rcond"]):
                        e.dma_start(out=self.FLT[:], in_=src_ap).then_inc(self.psem, 16)
                        e.reg_add(rg["rcnt"], rg["rcnt"], 16)
                        e.wait_ge(self.psem, rg["rcnt"])
                        e.reg_load(rg["rflag"], self.FLT[0:1, 0:1])
                        e.reg_alu(rg["rcond"], rg["rflag"], rg["rnon"], ALU.bitwise_xor)
            return e.nop()
        return P.op("gpsimd", fn, wait=[self.toks["regs"]])

    def ph_f1(self):
        P, AR = self.P, self.AR
        AR.reset()
        LFM = AR.take(128, 64 * 512, BF16)
        WG = [AR.take(128, 16 * 256, BF16) for _ in range(2)]
        WU = [AR.take(128, 16 * 256, BF16) for _ in range(2)]
        SG = AR.take(128, 512, F32)
        HB = AR.take(128, 2048, BF16)
        H2T = AR.take(128, 512, BF16)
        rt = {}
        t_pq = self.flag_poll(0)
        lfm_tok = {}
        for k in range(16):
            for tb in range(4):
                lfm_tok[(k, tb)] = P.op("sync", (lambda k_=k, t_=tb: lambda e: e.dma_start_transpose(
                    LFM[:, (k_ * 4 + t_) * 512:(k_ * 4 + t_ + 1) * 512],
                    self.LN1S.ap()[t_ * 512:(t_ + 1) * 512, k_ * 128:(k_ + 1) * 128]))(),
                    dma=True, stream="xp0")
        lfm = lambda k, tb: LFM[:, (k * 4 + tb) * 512:(k * 4 + tb + 1) * 512]
        if self.taps_on:
            for kq in range(4):
                self.tap(f"lfm{kq}", LFM[:, kq * 8192:(kq + 1) * 8192],
                         [lfm_tok[(k, tb)] for k in range(kq * 4, kq * 4 + 4) for tb in range(4)])
        for jb in range(32):
            u = jb % 2
            t_wg_l, t_wu_l = [], []
            for k in range(16):
                t_wg_l.append(P.op("sync", (lambda j_=jb, k_=k, b=WG[u]: lambda e: e.dma_start_transpose(
                    b[:, k_ * 256:(k_ + 1) * 256],
                    self.WQGU.ap()[j_ * 256:(j_ + 1) * 256, k_ * 128:(k_ + 1) * 128]))(),
                    wait=[t_pq, rt.get(("wgread", u))], dma=True, stream="xp0"))
                t_wu_l.append(P.op("sync", (lambda j_=jb, k_=k, b=WU[u]: lambda e: e.dma_start_transpose(
                    b[:, k_ * 256:(k_ + 1) * 256],
                    self.WQGU.ap()[HF + j_ * 256:HF + (j_ + 1) * 256, k_ * 128:(k_ + 1) * 128]))(),
                    wait=[t_pq, rt.get(("wuread", u))], dma=True, stream="xp0"))
            for jj in range(2):
                j = jb * 2 + jj
                for tb in range(4):
                    pg = self.PS[tb % 2]
                    pu = self.PS[2 + tb % 2]
                    t_g = t_u2 = None
                    for k in range(16):
                        t_g = P.op("tensor", (lambda k_=k, j_=jj, t_=tb, p=pg, b=WG[u]: lambda e: e.matmul(
                            p[:, 0:512], b[:, k_ * 256 + j_ * 128:k_ * 256 + (j_ + 1) * 128],
                            lfm(k_, t_), start=(k_ == 0), stop=(k_ == 15)))(),
                            wait=([t_wg_l[k], lfm_tok[(k, tb)]] + ([rt.get(("pg", tb % 2))] if k == 0 else [])),
                            silent=(k < 15))
                    for k in range(16):
                        t_u2 = P.op("tensor", (lambda k_=k, j_=jj, t_=tb, p=pu, b=WU[u]: lambda e: e.matmul(
                            p[:, 0:512], b[:, k_ * 256 + j_ * 128:k_ * 256 + (j_ + 1) * 128],
                            lfm(k_, t_), start=(k_ == 0), stop=(k_ == 15)))(),
                            wait=([t_wu_l[k], lfm_tok[(k, tb)]] + ([rt.get(("pu", tb % 2))] if k == 0 else [])),
                            silent=(k < 15))
                    rt[("wgread", u)] = t_g
                    rt[("wuread", u)] = t_u2
                    t_sg = P.op("scalar", (lambda p=pg: lambda e: e.activation(SG, p[:, 0:512], AF.Silu))(),
                                wait=[t_g, rt.get("sgread")])
                    rt[("pg", tb % 2)] = t_sg
                    if self.taps_on and j == 4 and tb == 0:
                        tq = P.op("vector", (lambda p=pg: lambda e: e.tensor_copy(HB[:, 0:512].bitcast(F32)[:, 0:256] if False else SG, p[:, 0:512]))(), wait=[t_g], silent=True)
                        ttap = self.tap("gate40", SG, [t_sg])
                        if ttap is not None:
                            rt["sgread"] = ttap
                    t_h = P.op("vector", (lambda p=pu, t_=tb: lambda e: e.tensor_tensor(
                        HB[:, t_ * 512:(t_ + 1) * 512], SG, p[:, 0:512], ALU.mult))(),
                        wait=[t_sg, t_u2, rt.get(("hbread", tb))])
                    rt[("pu", tb % 2)] = t_h
                    rt["sgread"] = t_h
                    t_h2 = P.op("scalar", (lambda t_=tb: lambda e: e.activation(
                        H2T, HB[:, t_ * 512:(t_ + 1) * 512], AF.Square))(),
                        wait=[t_h, rt.get("h2read")])
                    t_sq = P.op("tensor", (lambda t_=tb, j_=j: lambda e: e.matmul(
                        self.PS[4 + t_][0:1, 0:512], self.ONEB[:], H2T,
                        start=(j_ == 0), stop=(j_ == 63)))(),
                        wait=[t_h2, self.toks["oneb"]])
                    rt["h2read"] = t_sq
                t_hd = P.op("sync", (lambda j_=j: lambda e: e.dma_start(
                    out=self.HS.ap()[j_ * 128:(j_ + 1) * 128, :], in_=HB))(),
                    wait=[rt.get(("pu", 0)), rt.get(("pu", 1))], dma=True)
                for tb in range(4):
                    rt[("hbread", tb)] = t_hd
        t_row = []
        for tb in range(4):
            t_ev = P.op("vector", (lambda t_=tb: lambda e: e.tensor_copy(
                self.BR[0:1, t_ * 512:(t_ + 1) * 512], self.PS[4 + t_][0:1, 0:512]))(),
                wait=[rt["h2read"], rt.get("brrow")])
            t_row.append(t_ev)
        t_r1 = P.op("vector", lambda e: e.tensor_scalar_mul(
            self.BR[0:1, 2048:4096], self.BR[0:1, 0:2048], 1.0 / HF), wait=t_row)
        t_r2 = P.op("scalar", lambda e: e.activation(
            self.BR[0:1, 0:2048], self.BR[0:1, 2048:4096], AF.Ln,
            bias=self.COLS[0:1, 10:11]), wait=[t_r1, self.t_eps_ffn])
        t_r3 = P.op("scalar", lambda e: e.activation(
            self.BR[0:1, 2048:4096], self.BR[0:1, 0:2048], AF.Exp, scale=-0.5), wait=[t_r2])
        t_rd = P.op("gpsimd", lambda e: e.dma_start(out=self.RSS.ap(),
                    in_=self.BR[0:1, 2048:4096]), wait=[t_r3], dma=True)
        self.t_rst2 = P.op("sync", lambda e: e.dma_start(
            out=self.RST2[:], in_=self.RSS.ap().rearrange("o (c p) -> (o p) c", p=128),
            allow_slow_non_contiguous=True),
            wait=[t_rd], dma=True)

    def ph_f2(self):
        P, AR = self.P, self.AR
        AR.reset()
        HSL = AR.take(128, 64 * 512, BF16)
        WPT = [AR.take(128, 1024, BF16) for _ in range(2)]
        R2 = [AR.take(128, 2048, F32) for _ in range(4)]
        LNTF = AR.take(128, 4 * 2048, BF16)
        OUT = AR.take(128, 2048, F32)
        M1 = self.COLS[:, 4:5]
        Q1 = self.COLS[:, 5:6]
        T2 = self.COLS[:, 6:7]
        VAR = self.COLS[:, 7:8]
        RSD = self.COLS[:, 8:9]
        SQ2 = OUT
        rt = {}
        for tblk in range(4):
            hs_tok = {}
            for j in range(64):
                hs_tok[j] = P.op("sync", (lambda j_=j, t0=tblk: lambda e: e.dma_start(
                    out=HSL[:, j_ * 512:(j_ + 1) * 512],
                    in_=self.HS.ap()[j_ * 128:(j_ + 1) * 128, t0 * 512:(t0 + 1) * 512]))(),
                    wait=[rt.get(("hslread", j))], dma=True)
            for dh in range(2):
                t_mm_last = {}
                for j in range(64):
                    u = j % 2
                    t_wp = P.op("sync", (lambda j_=j, d_=dh, b=WPT[u]: lambda e: e.dma_start_transpose(
                        b, self.WQP.ap()[d_ * 1024:(d_ + 1) * 1024, j_ * 128:(j_ + 1) * 128]))(),
                        wait=[rt.get(("wptread", u))], dma=True, stream="xp0")
                    for tt in range(4):
                        for ob in range(2):
                            pi = tt * 2 + ob
                            t_mm = P.op("tensor", (lambda j_=j, t_=tt, o_=ob, p=self.PS[pi], b=WPT[u]: lambda e: e.matmul(
                                p[:, 0:512], HSL[:, j_ * 512 + t_ * 128:j_ * 512 + t_ * 128 + 128],
                                b[:, o_ * 512:(o_ + 1) * 512],
                                start=(j_ == 0), stop=(j_ == 63)))(),
                                wait=([t_wp, hs_tok[j]] + ([rt.get(("pf", pi))] if j == 0 else [])),
                                silent=(j < 63))
                            t_mm_last[pi] = t_mm
                    rt[("wptread", u)] = t_mm_last[7]
                for j in range(64):
                    rt[("hslread", j)] = t_mm_last[7]
                for tt in range(4):
                    if dh == 0:
                        self._lf_tok = getattr(self, "_lf_tok", {})
                        self._lf_tok[tt] = P.op("sync", (lambda t0=tblk, t_=tt: lambda e: e.dma_start(
                            out=LNTF[:, t_ * 2048:(t_ + 1) * 2048],
                            in_=self.LN1S.ap()[t0 * 512 + t_ * 128:t0 * 512 + t_ * 128 + 128, :]))(),
                            wait=[rt.get(("lntfread", tt))], dma=True)
                    for ob in range(2):
                        pi = tt * 2 + ob
                        col = dh * 1024 + ob * 512
                        t_ev = P.op("vector", (lambda p=self.PS[pi], r=R2[tt], c=col, t0=tblk, t_=tt: lambda e: e.scalar_tensor_tensor(
                            r[:, c:c + 512], p[:, 0:512], self.RST2[:, t0 * 4 + t_:t0 * 4 + t_ + 1],
                            LNTF[:, t_ * 2048 + c:t_ * 2048 + c + 512], ALU.mult, ALU.add))(),
                            wait=[t_mm_last[pi], self.t_rst2, self._lf_tok[tt],
                                  rt.get(("r2read", tt))])
                        rt[("pf", pi)] = t_ev
                        rt[("evtok", tt, dh, ob)] = t_ev
            if self.taps_on and tblk == 0:
                tr2 = self.tap("r2t0", R2[0], [rt[("evtok", 0, d_, o_)] for d_ in range(2) for o_ in range(2)])
                if tr2 is not None:
                    rt[("r2read", 0)] = [tr2, rt.get(("r2read", 0))]
            for tt in range(4):
                evs = [rt[("evtok", tt, d_, o_)] for d_ in range(2) for o_ in range(2)]
                t_m = P.op("vector", (lambda r=R2[tt]: lambda e: e.tensor_reduce(
                    M1.rearrange("p (a o) -> p a o", o=1),
                    r.rearrange("p (a k) -> p a k", a=1), AX, ALU.add))(),
                    wait=evs + [rt.get("m1read")])
                t_sq = P.op("scalar", (lambda r=R2[tt]: lambda e: e.activation(SQ2, r, AF.Square))(),
                            wait=evs + [rt.get("sqread"), rt.get("outread")])
                t_q = P.op("vector", lambda e: e.tensor_reduce(
                    Q1.rearrange("p (a o) -> p a o", o=1),
                    SQ2.rearrange("p (a k) -> p a k", a=1), AX, ALU.add),
                    wait=[t_sq, rt.get("q1read")])
                rt["sqread"] = t_q
                t_msc = P.op("vector", lambda e: e.tensor_scalar_mul(M1, M1, 1.0 / DM),
                             wait=[t_m, rt.get("m1bread")])
                rt["m1read"] = t_msc
                t_mm2 = P.op("vector", lambda e: e.tensor_tensor(
                    T2, M1, M1, ALU.mult), wait=[t_msc, rt.get("t2read")])
                t_var = P.op("vector", lambda e: e.scalar_tensor_tensor(
                    VAR, Q1, 1.0 / DM, T2, ALU.mult, ALU.subtract),
                    wait=[t_mm2, t_q, rt.get("varread")])
                rt["q1read"] = t_var
                rt["t2read"] = t_var
                t_ln = P.op("scalar", lambda e: e.activation(VAR, VAR, AF.Ln, bias=self.COLS[:, 9:10]),
                            wait=[t_var, self.t_eps_ln])
                t_ex = P.op("scalar", lambda e: e.activation(RSD, VAR, AF.Exp, scale=-0.5),
                            wait=[t_ln, rt.get("rsdread")])
                rt["varread"] = t_ex
                t_out = P.op("vector", (lambda r=R2[tt]: lambda e: e.tensor_scalar(
                    OUT, r, M1, RSD, ALU.subtract, ALU.mult))(),
                    wait=[t_msc, t_ex, rt.get("outread")])
                rt["rsdread"] = t_out
                rt["m1bread"] = t_out
                rt[("r2read", tt)] = t_out
                t_st = P.op("sync", (lambda t0=tblk, t_=tt: lambda e: e.dma_start(
                    out=self.outt.ap()[t0 * 512 + t_ * 128:t0 * 512 + t_ * 128 + 128, :],
                    in_=OUT))(), wait=[t_out], dma=True)
                rt["outread"] = t_st
                rt[("lntfread", tt)] = t_st

    def ph_dumps(self):
        P = self.P
        for name in self.dumps:
            src = getattr(self, name)
            dst = self.dump_t[name]
            sap, dap = src.ap(), dst.ap()
            if len(sap.shape) == 3:
                sap = sap.rearrange("a b c -> (a b) c")
                dap = dap.rearrange("a b c -> (a b) c")
            rows = sap.shape[0]
            step = 512
            for r0 in range(0, rows, step):
                r1 = min(r0 + step, rows)
                P.op("sync", (lambda s=sap[r0:r1, :], d=dap[r0:r1, :]: lambda e: e.dma_start(
                    out=d, in_=s))(), dma=True)


def build_nc(dumps=(), stop_after=None, taps_on=False):
    kb = KB(dumps=dumps, taps_on=taps_on)
    nc = kb.build(stop_after=stop_after)
    return nc, kb


def kernel(L0_attn_u, L0_attn_W_in, L0_attn_conv_w, L0_attn_conv_b,
           L0_attn_dt_bias, L0_attn_A_log, L0_attn_D, L0_attn_norm_weight,
           L0_attn_W_out, L0_ffn_w_gate_up, L0_ffn_w_proj,
           _dumps=(), _stop_after=None):
    nonce = int(np.random.randint(1, 2 ** 31 - 1))
    maps = make_in_maps(L0_attn_u, L0_attn_W_in, L0_attn_conv_w, L0_attn_conv_b,
                        L0_attn_dt_bias, L0_attn_A_log, L0_attn_D,
                        L0_attn_norm_weight, L0_attn_W_out, L0_ffn_w_gate_up,
                        L0_ffn_w_proj, nonce)
    nc, kb = build_nc(dumps=_dumps, stop_after=_stop_after)
    res = run_bass_kernel_spmd(nc, maps, core_ids=list(range(8)))
    out = np.zeros((4, 4096, 2048), np.float32)
    for c in range(8):
        s, par = c // 2, c % 2
        out[s, par * 2048:(par + 1) * 2048, :] = np.asarray(res.results[c]["outp"])
    if _dumps or _stop_after:
        return out, res
    return out
